# version 10
# baseline (speedup 1.0000x reference)
import numpy as np
import ml_dtypes
from contextlib import ExitStack

import concourse.bass as bass
import concourse.mybir as mybir
from concourse.bass_utils import run_bass_kernel_spmd

F32 = mybir.dt.float32
BF16 = mybir.dt.bfloat16
AF = mybir.ActivationFunctionType
ALU = mybir.AluOpType
AX = mybir.AxisListType

P = 128
D = 2048
SEQ = 2048
H = 16
NK = 16
HID = 8192
NMH = 64
T = 512
NG = SEQ // T
LN_EPS = 1e-5
RMS_EPS = float(np.finfo(np.float32).eps)
THR = 0.5 + 2.0 ** -10
ARENA = 48624
import os
PHASES = int(os.environ.get("KN_PHASES", "5"))
A1A_STEPS = int(os.environ.get("KN_A1A", "9"))
DBG = int(os.environ.get("KN_DBG", "0"))
OPT = int(os.environ.get("KN_OPT", "3"))
PROJ_STEPS = int(os.environ.get("KN_PROJ", "9"))

ENGINES = ("tensor", "vector", "scalar", "gpsimd", "sync")

PSUM_KEYS = frozenset(
    ["psum", "psq", "hq", "pq0", "pq1", "pk0", "pk1", "hkv", "qc", "qs",
     "gc", "gs", "vp0", "vp1", "pa0", "pa1", "ssum", "ssq"]
    + [f"pb{i}" for i in range(8)] + ["pbc5", "pbc6", "pbc7"])


class Em:

    NDS = 8

    def __init__(self, xkeys=PSUM_KEYS):
        self.ops = {e: [] for e in ENGINES}
        self.cnt = {e: 0 for e in ENGINES}
        self.dq = {e: [0] * self.NDS for e in ENGINES}
        self.drr = {e: 0 for e in ENGINES}
        self.last_w = {}
        self.readers = {}
        self.cur = {e: {} for e in ENGINES}
        self.xkeys = xkeys
        self.last_rd = {}

    def _need(self, eng, kind, name, count, wl):
        if count <= 0:
            return
        if kind == "e" and name == eng and eng == "tensor":
            return
        if self.cur[eng].get((kind, name), 0) >= count:
            return
        key = (kind, name)
        wl[key] = max(wl.get(key, 0), count)

    def _deps(self, eng, reads, writes):
        wl = {}
        for r in reads:
            lw = self.last_w.get(r)
            if lw:
                self._need(eng, lw[0], lw[1], lw[2], wl)
        for w in writes:
            lw = self.last_w.get(w)
            if lw:
                self._need(eng, lw[0], lw[1], lw[2], wl)
            for (k, n), c in self.readers.get(w, {}).items():
                self._need(eng, k, n, c, wl)
        for (k, n), c in wl.items():
            self.cur[eng][(k, n)] = max(self.cur[eng].get((k, n), 0), c)
        return wl

    def op(self, eng, fn, reads=(), writes=()):
        wl = self._deps(eng, reads, writes)
        for r in reads:
            if r in self.xkeys:
                lr = self.last_rd.get(r)
                if lr and lr[1] != eng:
                    self._need(eng, lr[0], lr[1], lr[2], wl)
                    self.cur[eng][(lr[0], lr[1])] = max(
                        self.cur[eng].get((lr[0], lr[1]), 0), lr[2])
        self.cnt[eng] += 1
        c = self.cnt[eng]
        for r in reads:
            if r in self.xkeys:
                self.last_rd[r] = ("e", eng, c)

        def run(h, sems, wl=wl, fn=fn, eng=eng):
            for (k, n), v in wl.items():
                h.wait_ge(sems[(k, n)], v)
            fn(h).then_inc(sems[("e", eng)], 1)

        self.ops[eng].append(run)
        for w in writes:
            self.last_w[w] = ("e", eng, c)
            self.readers[w] = {}
        for r in reads:
            self.readers.setdefault(r, {})[("e", eng)] = c

    def dma(self, q, fn, reads=(), writes=()):
        wl = self._deps(q, reads, writes)
        s = self.drr[q]
        self.drr[q] = (s + 1) % self.NDS
        self._need(q, "d", (q, s), self.dq[q][s], wl)
        self.cur[q][("d", (q, s))] = max(
            self.cur[q].get(("d", (q, s)), 0), self.dq[q][s])
        self.dq[q][s] += 16
        c = self.dq[q][s]

        def run(h, sems, wl=wl, fn=fn, q=q, s=s):
            for (k, n), v in wl.items():
                h.wait_ge(sems[(k, n)], v)
            fn(h).then_inc(sems[("d", (q, s))], 16)

        self.ops[q].append(run)
        for w in writes:
            self.last_w[w] = ("d", (q, s), c)
            self.readers[w] = {}
        for r in reads:
            self.readers.setdefault(r, {})[("d", (q, s))] = c

    def fence(self):
        snap_e = dict(self.cnt)
        snap_d = {e: list(v) for e, v in self.dq.items()}
        for e in ENGINES:
            wl = {}
            for e2 in ENGINES:
                if e2 != e:
                    self._need(e, "e", e2, snap_e[e2], wl)
                for s in range(self.NDS):
                    self._need(e, "d", (e2, s), snap_d[e2][s], wl)
            for (k, n), c in wl.items():
                self.cur[e][(k, n)] = max(self.cur[e].get((k, n), 0), c)

            def run(h, sems, wl=wl):
                for (k, n), v in wl.items():
                    h.wait_ge(sems[(k, n)], v)

            self.ops[e].append(run)


class Arena:
    def __init__(self, ap):
        self.ap = ap
        self.off = 0

    def reset(self, base=0):
        self.off = base

    def _take(self, units):
        units = (units + 15) & ~15
        o = self.off
        self.off += units
        assert self.off <= ARENA, f"arena overflow: {self.off} > {ARENA}"
        return o

    def f32(self, n):
        o = self._take(n)
        return self.ap[:, o:o + n]

    def bf16(self, n):
        o = self._take((n + 1) // 2)
        return self.ap[:, o:o + (n + 1) // 2].bitcast(BF16)[:, :n]


def build_nc():
    nc = bass.Bass("TRN2", target_bir_lowering=False)

    xT = nc.dram_tensor("xT", [D, SEQ], F32, kind="ExternalInput").ap()
    wqT = nc.dram_tensor("wqT", [D, D], F32, kind="ExternalInput").ap()
    wkvT = nc.dram_tensor("wkvT", [D, D], F32, kind="ExternalInput").ap()
    woT = nc.dram_tensor("woT", [D, D], F32, kind="ExternalInput").ap()
    wfc = nc.dram_tensor("wfc", [HID, D], F32, kind="ExternalInput").ap()
    wproj = nc.dram_tensor("wproj", [D, HID], F32, kind="ExternalInput").ap()
    cdft = nc.dram_tensor("cdft", [P, 4 * P], BF16, kind="ExternalInput").ap()
    chsel = nc.dram_tensor("chsel", [P, H * H], BF16, kind="ExternalInput").ap()
    cones = nc.dram_tensor("cones", [P, 1], BF16, kind="ExternalInput").ap()
    conesr = nc.dram_tensor("conesr", [1, P], BF16, kind="ExternalInput").ap()
    cmisc = nc.dram_tensor("cmisc", [P, 4], F32, kind="ExternalInput").ap()

    out_xT = nc.dram_tensor("out_xT", [D, SEQ], F32, kind="ExternalOutput").ap()

    _ikind = "ExternalOutput" if DBG else "Internal"
    fc_t = nc.dram_tensor("fc_t", [HID, D], BF16, kind=_ikind).ap()
    pj_t = nc.dram_tensor("pj_t", [D, HID], BF16, kind="Internal").ap()
    pjT_t = nc.dram_tensor("pjT_t", [HID, D], BF16, kind=_ikind).ap()
    x1_scr = nc.dram_tensor("x1_scr", [D, SEQ], BF16, kind=_ikind).ap()
    qn_scr = nc.dram_tensor("qn_scr", [D, SEQ], BF16, kind=_ikind).ap()
    a_scr = nc.dram_tensor("a_scr", [D, SEQ], BF16, kind=_ikind).ap()
    b_scr = nc.dram_tensor("b_scr", [D, SEQ], BF16, kind=_ikind).ap()
    x2_scr = nc.dram_tensor("x2_scr", [D, SEQ], BF16, kind=_ikind).ap()
    r_scr = nc.dram_tensor("r_scr", [10, SEQ], BF16, kind="Internal").ap()
    rq_scr = nc.dram_tensor("rq_scr", [H, SEQ], BF16, kind="Internal").ap()
    rkv_scr = nc.dram_tensor("rkv_scr", [H, SEQ], BF16, kind="Internal").ap()

    em = Em()
    es = ExitStack()
    with es:
        sems = {}
        for e in ENGINES:
            sems[("e", e)] = es.enter_context(nc.semaphore(f"se_{e}"))
            for s in range(Em.NDS):
                sems[("d", (e, s))] = es.enter_context(
                    nc.semaphore(f"sd_{e}{s}"))

        tdft = es.enter_context(nc.sbuf_tensor("tdft", [P, 4 * P], BF16))
        thsel = es.enter_context(nc.sbuf_tensor("thsel", [P, H * H], BF16))
        tones = es.enter_context(nc.sbuf_tensor("tones", [P, 1], BF16))
        tonesr = es.enter_context(nc.sbuf_tensor("tonesr", [1, P], BF16))
        tmisc = es.enter_context(nc.sbuf_tensor("tmisc", [P, 4], F32))
        carry = es.enter_context(nc.sbuf_tensor("carry", [P, H], F32))
        rows = es.enter_context(nc.sbuf_tensor("rows", [P, 8 * T], F32))
        arena_t = es.enter_context(nc.sbuf_tensor("arena", [P, ARENA], F32))
        pall = es.enter_context(nc.psum_tensor("pall", [P, 4096], F32))

        ar = Arena(arena_t[:, :])
        tCf = tdft[:, 0 * P:1 * P]
        tSf = tdft[:, 1 * P:2 * P]
        tCi = tdft[:, 2 * P:3 * P]
        tSi = tdft[:, 3 * P:4 * P]
        r_m = rows[0:1, 0 * T:1 * T]
        r_v = rows[0:1, 1 * T:2 * T]
        r_t1 = rows[0:1, 2 * T:3 * T]
        onesrow = tonesr[0:1, :]
        r_rstd = rows[0:1, 3 * T:3 * T + T // 2].bitcast(BF16)
        r_mr = rows[0:1, 4 * T:4 * T + T // 2].bitcast(BF16)
        r_h16 = rows[0:16, 6 * T:7 * T]
        r_h16b = rows[0:16, 5 * T:5 * T + T // 2].bitcast(BF16)
        eps_ln = tmisc[0:1, 0:1]
        eps_rms = tmisc[0:1, 1:2]
        zeroT = tmisc[:, 2:3].broadcast_to([P, T])

        def bk(i):
            return pall[:, i * 512:(i + 1) * 512]

        def bkrow(i):
            return pall[0:1, i * 512:(i + 1) * 512]

        def bk16(i):
            return pall[0:16, i * 512:(i + 1) * 512]

        for nm, dst, src in [("cdft", tdft, cdft), ("chsel", thsel, chsel),
                             ("cones", tones, cones), ("conesr", tonesr, conesr),
                             ("cmisc", tmisc, cmisc)]:
            em.dma("sync", lambda h, d=dst, s=src: h.dma_start(out=d[:, :], in_=s[:, :]),
                   writes=(nm,))

        def c2(tile, c):
            return tile[:, c * T:(c + 1) * T]

        def r3(tile):
            return tile.rearrange("p (c t) -> p c t", t=T)

        def xsl(src, g):
            return src.rearrange("(c p) t -> p c t", p=P)[:, :, g * T:(g + 1) * T]

        def ln_stats(src_tile, src_key, sq_bufs, sq_keys, sum_row, sq_row,
                     sum_key, sq_key):
            nb = len(sq_bufs)
            pend = []

            def flush(k, i):
                em.op("tensor", lambda h, k=k, i=i: h.matmul(
                    sq_row, tones[:, :], sq_bufs[i],
                    start=(k == 0), stop=(k == NK - 1)),
                    reads=(sq_keys[i], "cones"), writes=(sq_key,))
            for k in range(NK):
                i = k % nb
                em.op("tensor", lambda h, k=k: h.matmul(
                    sum_row, tones[:, :], c2(src_tile, k),
                    start=(k == 0), stop=(k == NK - 1)),
                    reads=(src_key, "cones"), writes=(sum_key,))
                em.op("scalar", lambda h, k=k, i=i: h.activation(
                    sq_bufs[i], c2(src_tile, k), AF.Square),
                    reads=(src_key,), writes=(sq_keys[i],))
                pend.append((k, i))
                if len(pend) > nb - 2:
                    flush(*pend.pop(0))
            while pend:
                flush(*pend.pop(0))

        def ln_rows(tag, sum_row, sq_row, st_keys, denom, scr_row, gsl):
            em.op("vector", lambda h: h.tensor_scalar(
                r_m, sum_row, 1.0 / denom, None, op0=ALU.mult),
                reads=st_keys, writes=("r_m",))
            em.op("vector", lambda h: h.tensor_tensor(
                r_t1, r_m, r_m, op=ALU.mult),
                reads=("r_m",), writes=("r_t1",))
            em.op("vector", lambda h: h.scalar_tensor_tensor(
                r_v, sq_row, 1.0 / denom, r_t1,
                op0=ALU.mult, op1=ALU.subtract),
                reads=st_keys + ("r_t1",), writes=("r_v",))
            em.op("scalar", lambda h: h.activation(
                r_t1, r_v, AF.Ln, bias=eps_ln),
                reads=("r_v", "cmisc"), writes=("r_t1",))
            em.op("scalar", lambda h: h.activation(
                r_rstd, r_t1, AF.Exp, scale=-0.5),
                reads=("r_t1",), writes=("r_rstd",))
            em.op("vector", lambda h: h.tensor_tensor(
                r_mr, r_m, r_rstd, op=ALU.mult),
                reads=("r_m", "r_rstd"), writes=("r_mr",))
            return "r"

        def bc_mm2(dst, dst_key, bank0, bank1, bk0_key, bk1_key, scr_key):
            em.op("tensor", lambda h: h.matmul(
                bank0, onesrow, r_rstd, start=True, stop=True),
                reads=("r_rstd", "conesr"), writes=(bk0_key,))
            em.op("tensor", lambda h: h.matmul(
                bank1, onesrow, r_mr, start=True, stop=True),
                reads=("r_mr", "conesr"), writes=(bk1_key,))
            em.op("vector", lambda h: h.tensor_copy(
                dst[:, 0:T], bank0), reads=(bk0_key,), writes=(dst_key,))
            em.op("vector", lambda h: h.tensor_copy(
                dst[:, T:2 * T], bank1), reads=(bk1_key,), writes=(dst_key,))

        def bc_read2(dst, dst_key, scr_row, gsl, scr_key):
            em.dma("scalar", lambda h: h.dma_start(
                out=r3(dst),
                in_=r_scr[scr_row:scr_row + 2, gsl]
                .unsqueeze(0).broadcast_to([P, 2, T])),
                reads=(scr_key,), writes=(dst_key,))

        def ln_apply(src, src_key, tmp, tmp_key, dst, dst_key, bc, bc_key, n=NK):
            em.op("vector", lambda h: h.tensor_tensor(
                r3(tmp), r3(src),
                bc[:, 0:T].unsqueeze(1).broadcast_to([P, n, T]), op=ALU.mult),
                reads=(src_key, bc_key), writes=(tmp_key,))
            em.op("vector", lambda h: h.tensor_tensor(
                r3(dst), r3(tmp),
                bc[:, T:2 * T].unsqueeze(1).broadcast_to([P, n, T]),
                op=ALU.subtract),
                reads=(tmp_key, bc_key), writes=(dst_key,))

        def proj(wsb, wkeys, src, src_key, pbks, pbkeys, dst, dst_key,
                 hsq_bank, hsq_key, sqb, sqkeys):
            w3 = wsb.rearrange("p (k m) -> p k m", m=D)
            sq_hold = [None] * NK
            pend = []

            def flush_stat():
                m, i = pend.pop(0)
                em.op("tensor", lambda h, m=m, i=i: h.matmul(
                    hsq_bank, thsel[:, m * H:(m + 1) * H], sqb[i],
                    start=(m == 0), stop=(m == NK - 1)),
                    reads=(sqkeys[i], "chsel"), writes=(hsq_key,))

            nsq = len(sqkeys)
            for m in range(NK):
                i = m % nsq

                def mmf(h, m=m, i=i):
                    last = None
                    for k in range(NK):
                        last = h.matmul(
                            pbks[i % 2], w3[:, k, m * P:(m + 1) * P],
                            c2(src, k), start=(k == 0), stop=(k == NK - 1))
                    return last
                em.op("tensor", mmf, reads=(src_key,) + wkeys,
                      writes=(pbkeys[i % 2],))
                em.op("scalar", lambda h, i=i: h.activation(
                    sqb[i], pbks[i % 2], AF.Square),
                    reads=(pbkeys[i % 2],), writes=(sqkeys[i],))
                em.op("scalar", lambda h, m=m, i=i: h.copy(
                    c2(dst, m), pbks[i % 2]),
                    reads=(pbkeys[i % 2],), writes=(dst_key,))
                pend.append((m, i))
                if len(pend) > min(2, nsq - 2) if nsq > 2 else len(pend) > 0:
                    flush_stat()
            while pend:
                flush_stat()

        def head_rsqrt(hsq_bank, hsq_key, scr, scr_key, gsl, exp_tile, exp_key):
            em.op("scalar", lambda h: h.activation(
                r_h16, hsq_bank, AF.Ln, scale=128.0),
                reads=(hsq_key,), writes=("r_h16",))
            em.op("scalar", lambda h: h.activation(
                r_h16b, r_h16, AF.Exp, scale=-0.5),
                reads=("r_h16",), writes=("r_h16b",))
            em.dma("scalar", lambda h: h.dma_start(
                out=scr[:, gsl], in_=r_h16b),
                reads=("r_h16b",), writes=(scr_key,))
            em.dma("scalar", lambda h: h.dma_start(
                out=r3(exp_tile),
                in_=scr[:, gsl].unsqueeze(0).broadcast_to([P, H, T])),
                reads=(scr_key,), writes=(exp_key,))

        ar.reset()
        p_w = [ar.f32(D) for _ in range(2)]
        p_ab = [ar.bf16(D) for _ in range(2)]
        p_sgn = [ar.bf16(D) for _ in range(2)]
        p_msk = [ar.bf16(D) for _ in range(2)]
        p_ss = [ar.bf16(D) for _ in range(2)]
        p_gs = [ar.f32(32) for _ in range(2)]
        p_s = [ar.bf16(32) for _ in range(2)]
        p_T = [ar.f32(32) for _ in range(2)]
        pt_buf = [ar.bf16(D) for _ in range(1)]
        PRES = ar.off

        ar.reset(PRES)
        wq_sb = ar.bf16(NK * D)
        x_bf1 = ar.bf16(NK * T)
        sq_t = [ar.bf16(T) for _ in range(4)]
        x1_bf = ar.bf16(NK * T)
        q_bf = ar.bf16(NK * T)
        qn_bf = x1_bf
        bc2g = [ar.bf16(2 * T) for _ in range(2)]

        for kq in range(4):
            em.dma("gpsimd", lambda h, kq=kq: h.dma_start(
                out=wq_sb.rearrange("p (k m) -> p k m", m=D)[:, 4 * kq:4 * kq + 4, :],
                in_=wqT.rearrange("(k p) m -> p k m", p=P)[:, 4 * kq:4 * kq + 4, :]),
                writes=(f"wq{kq}",))
        WQK = tuple(f"wq{i}" for i in range(4))

        def tern_tile(wap, outap, r0, c0, ldi, tag, ti):
            def thunk():
                j = ldi
                kw, kab, ksg = f"p_w{j}", f"p_ab{j}", f"p_sgn{j}"
                kms, kss = f"p_msk{j}", f"p_ss{j}"
                kgs, ks, kT = f"p_gs{j}", f"p_s{j}", f"p_T{j}"
                em.dma("sync", lambda h: h.dma_start(
                    out=p_w[j][:, :], in_=wap[r0:r0 + P, c0:c0 + D]),
                    writes=(kw,))
                em.op("scalar", lambda h: h.activation(
                    p_ab[j][:, :], p_w[j][:, :], AF.Abs),
                    reads=(kw,), writes=(kab,))
                em.op("scalar", lambda h: h.activation(
                    p_sgn[j][:, :], p_w[j][:, :], AF.Sign),
                    reads=(kw,), writes=(ksg,))
                em.op("vector", lambda h: h.tensor_reduce(
                    p_gs[j][:, :],
                    p_ab[j].rearrange("p (g e) -> p g e", e=64),
                    axis=AX.X, op=ALU.add),
                    reads=(kab,), writes=(kgs,))
                em.op("vector", lambda h: h.tensor_scalar(
                    p_s[j][:, :], p_gs[j][:, :], 1.0 / 64, 1e-8,
                    op0=ALU.mult, op1=ALU.max),
                    reads=(kgs,), writes=(ks,))
                em.op("vector", lambda h: h.tensor_scalar(
                    p_T[j][:, :], p_s[j][:, :], THR, None, op0=ALU.mult),
                    reads=(ks,), writes=(kT,))
                em.op("vector", lambda h: h.tensor_tensor(
                    p_msk[j].rearrange("p (g e) -> p g e", e=64),
                    p_ab[j].rearrange("p (g e) -> p g e", e=64),
                    p_T[j][:, :].unsqueeze(2).broadcast_to([P, 32, 64]),
                    op=ALU.is_gt),
                    reads=(kab, kT), writes=(kms,))
                em.op("vector", lambda h: h.tensor_tensor(
                    p_ss[j].rearrange("p (g e) -> p g e", e=64),
                    p_sgn[j].rearrange("p (g e) -> p g e", e=64),
                    p_s[j][:, :].unsqueeze(2).broadcast_to([P, 32, 64]),
                    op=ALU.mult),
                    reads=(ksg, ks), writes=(kss,))
                em.op("vector", lambda h: h.tensor_tensor(
                    p_ab[j][:, :], p_msk[j][:, :], p_ss[j][:, :],
                    op=ALU.mult),
                    reads=(kms, kss), writes=(kab,))
                em.dma("gpsimd", lambda h: h.dma_start(
                    out=outap[r0:r0 + P, c0:c0 + D], in_=p_ab[j][:, :]),
                    reads=(kab,), writes=(f"{tag}_{ti}",))
            return thunk

        pwork = []
        if PHASES >= 1:
            ti = 0
            for rt in range(HID // P):
                pwork.append(tern_tile(wfc, fc_t, rt * P, 0, ti % 2, "fct", rt))
                ti += 1
            tj = 0
            for rt in range(D // P):
                for ct in range(HID // D):
                    pwork.append(tern_tile(wproj, pj_t, rt * P, ct * D,
                                           ti % 2, "pjt", tj))
                    ti += 1
                    tj += 1

            def pjT_thunk(m):
                def thunk():
                    em.dma("sync", lambda h: h.dma_start(
                        out=pt_buf[0][:, :], in_=pj_t[:, m * P:(m + 1) * P],
                        transpose=True),
                        reads=tuple(f"pjt_{t}" for t in range(64)),
                        writes=("ptb",))
                    em.dma("gpsimd", lambda h: h.dma_start(
                        out=pjT_t[m * P:(m + 1) * P, :], in_=pt_buf[0][:, :]),
                        reads=("ptb",), writes=(f"pjTt{m}",))
                return thunk

            for m in range(NMH):
                pwork.append(pjT_thunk(m))

        def drain(n):
            for _ in range(min(n, len(pwork))):
                pwork.pop(0)()

        drain(10)
        ST_SUM, ST_SQ, HQ, PQ0, PQ1 = 0, 1, 2, 3, 4
        for g in range(NG if PHASES >= 2 else 0):
            gsl = slice(g * T, (g + 1) * T)
            pg = g % 2
            x_bf = x_bf1
            kx = "x_bf"
            rh_exp = x_bf
            bc2 = bc2g[pg]
            kb = f"bc2{pg}"
            em.dma("gpsimd", lambda h, g=g, x_bf=x_bf: h.dma_start(
                out=r3(x_bf), in_=xsl(xT, g)), writes=(kx,))
            if A1A_STEPS < 2:
                continue
            ln_stats(x_bf, kx, sq_t, ("sq0", "sq1", "sq2", "sq3"),
                     bkrow(ST_SUM), bkrow(ST_SQ), "psum", "psq")
            if A1A_STEPS < 3:
                continue
            ln_rows("ln1", bkrow(ST_SUM), bkrow(ST_SQ), ("psum", "psq"),
                    D, 0, gsl)
            if A1A_STEPS < 4:
                continue
            bc_mm2(bc2, kb, bk(5), bk(6), "pbc5", "pbc6", None)
            ln_apply(x_bf, kx, q_bf, "q_bf", x1_bf, "x1_bf", bc2, kb)
            drain(6)
            em.dma("gpsimd", lambda h, g=g: h.dma_start(
                out=xsl(x1_scr, g), in_=r3(x1_bf)),
                reads=("x1_bf",), writes=(f"x1s{g}",))
            if A1A_STEPS < 5:
                continue
            proj(wq_sb, WQK, x1_bf, "x1_bf", [bk(PQ0), bk(PQ1)],
                 ("pq0", "pq1"), q_bf, "q_bf", bk16(HQ), "hq", sq_t,
                 ("sq0", "sq1", "sq2", "sq3"))
            if A1A_STEPS < 6:
                continue
            drain(6)
            head_rsqrt(bk16(HQ), "hq", rq_scr, f"rq{g}", gsl, rh_exp, kx)
            em.op("vector", lambda h, rh_exp=rh_exp: h.tensor_tensor(
                qn_bf[:, :], q_bf[:, :], rh_exp[:, :], op=ALU.mult),
                reads=("q_bf", kx), writes=("x1_bf",))
            em.dma("gpsimd", lambda h, g=g: h.dma_start(
                out=xsl(qn_scr, g), in_=r3(qn_bf)),
                reads=("x1_bf",), writes=(f"qns{g}",))
            drain(6)

        em.fence()

        ar.reset(PRES)
        wkv_sb = ar.bf16(NK * D)
        x1g1 = ar.bf16(NK * T)
        kv_bf = ar.bf16(NK * T)
        kn_t = [ar.bf16(T) for _ in range(2)]
        cn_bf = ar.bf16(NK * T)
        sq2_t = [ar.bf16(T) for _ in range(2)]
        qn_t = [ar.bf16(T) for _ in range(2)]
        qc_t = ar.bf16(T)
        qs_t = ar.bf16(T)
        gc_t = ar.bf16(T)
        gs_t = ar.bf16(T)
        dt1 = ar.bf16(T)
        dt2 = ar.bf16(T)
        A_t = [ar.bf16(T) for _ in range(2)]
        B_t = [ar.bf16(T) for _ in range(2)]

        for kq in range(4 if PHASES >= 3 else 0):
            em.dma("gpsimd", lambda h, kq=kq: h.dma_start(
                out=wkv_sb.rearrange("p (k m) -> p k m", m=D)[:, 4 * kq:4 * kq + 4, :],
                in_=wkvT.rearrange("(k p) m -> p k m", p=P)[:, 4 * kq:4 * kq + 4, :]),
                writes=(f"wkv{kq}",))
        WKVK = tuple(f"wkv{i}" for i in range(4))


        em.op("vector", lambda h: h.memset(carry[:, :], 0.0), writes=("carry",))

        PK0, PK1, HKV, QC, QS, GC, GS = 0, 1, 2, 3, 4, 5, 6
        for g in range(NG if PHASES >= 3 else 0):
            gsl = slice(g * T, (g + 1) * T)
            pg = g % 2
            x1g = x1g1
            kx1 = "x1g"
            rkv_exp = x1g
            em.dma("sync", lambda h, g=g, x1g=x1g: h.dma_start(
                out=r3(x1g), in_=xsl(x1_scr, g)),
                reads=(f"x1s{g}",), writes=(kx1,))
            proj(wkv_sb, WKVK, x1g, kx1, [bk(PK0), bk(PK1)],
                 ("pk0", "pk1"), kv_bf, "kv_bf", bk16(HKV), "hkv", sq2_t,
                 ("sq20", "sq21"))
            drain(8)
            head_rsqrt(bk16(HKV), "hkv", rkv_scr, f"rkv{g}", gsl,
                       rkv_exp, kx1)
            for c in range(NK):
                ki = c % 2
                em.op("vector", lambda h, c=c, ki=ki, rkv_exp=rkv_exp:
                      h.tensor_tensor(
                    kn_t[ki][:, :], c2(kv_bf, c), c2(rkv_exp, c),
                    op=ALU.mult),
                    reads=("kv_bf", kx1), writes=(f"kn{ki}",))
                em.op("vector", lambda h, c=c, ki=ki: h.tensor_tensor_scan(
                    c2(cn_bf, c), kn_t[ki][:, :], zeroT, carry[:, c:c + 1],
                    op0=ALU.add, op1=ALU.add),
                    reads=(f"kn{ki}", "carry", "cmisc"), writes=("cn_bf",))
                em.op("vector", lambda h, c=c: h.tensor_copy(
                    carry[:, c:c + 1], c2(cn_bf, c)[:, T - 1:T]),
                    reads=("cn_bf",), writes=("carry",))
            drain(8)
            for c in range(NK):
                i = c % 2
                em.dma("scalar", lambda h, c=c, i=i, gsl=gsl: h.dma_start(
                    out=qn_t[i][:, :], in_=qn_scr[c * P:(c + 1) * P, gsl]),
                    reads=(f"qns{g}",), writes=(f"qnt{i}",))
                em.op("tensor", lambda h, i=i: h.matmul(
                    bk(QC), tCf, qn_t[i][:, :], start=True, stop=True),
                    reads=(f"qnt{i}", "cdft"), writes=("qc",))
                em.op("tensor", lambda h, i=i: h.matmul(
                    bk(QS), tSf, qn_t[i][:, :], start=True, stop=True),
                    reads=(f"qnt{i}", "cdft"), writes=("qs",))
                em.op("tensor", lambda h, c=c: h.matmul(
                    bk(GC), tCf, c2(cn_bf, c), start=True, stop=True),
                    reads=("cn_bf", "cdft"), writes=("gc",))
                em.op("tensor", lambda h, c=c: h.matmul(
                    bk(GS), tSf, c2(cn_bf, c), start=True, stop=True),
                    reads=("cn_bf", "cdft"), writes=("gs",))
                em.op("scalar", lambda h: h.copy(qc_t[:, :], bk(QC)),
                      reads=("qc",), writes=("qc_t",))
                em.op("scalar", lambda h: h.copy(qs_t[:, :], bk(QS)),
                      reads=("qs",), writes=("qs_t",))
                em.op("vector", lambda h: h.tensor_copy(gc_t[:, :], bk(GC)),
                      reads=("gc",), writes=("gc_t",))
                em.op("vector", lambda h: h.tensor_copy(gs_t[:, :], bk(GS)),
                      reads=("gs",), writes=("gs_t",))
                em.op("vector", lambda h, i=i: h.tensor_tensor(
                    A_t[i][:, :], qc_t[:, :], gc_t[:, :], op=ALU.mult),
                    reads=("qc_t", "gc_t"), writes=(f"A{i}",))
                em.op("vector", lambda h: h.tensor_tensor(
                    dt1[:, :], qs_t[:, :], gs_t[:, :], op=ALU.mult),
                    reads=("qs_t", "gs_t"), writes=("dt1",))
                em.op("vector", lambda h, i=i: h.tensor_tensor(
                    A_t[i][:, :], A_t[i][:, :], dt1[:, :], op=ALU.subtract),
                    reads=(f"A{i}", "dt1"), writes=(f"A{i}",))
                em.op("vector", lambda h, i=i: h.tensor_tensor(
                    B_t[i][:, :], qc_t[:, :], gs_t[:, :], op=ALU.mult),
                    reads=("qc_t", "gs_t"), writes=(f"B{i}",))
                em.op("vector", lambda h: h.tensor_tensor(
                    dt2[:, :], qs_t[:, :], gc_t[:, :], op=ALU.mult),
                    reads=("qs_t", "gc_t"), writes=("dt2",))
                em.op("vector", lambda h, i=i: h.tensor_tensor(
                    B_t[i][:, :], B_t[i][:, :], dt2[:, :], op=ALU.add),
                    reads=(f"B{i}", "dt2"), writes=(f"B{i}",))
                em.dma("gpsimd", lambda h, c=c, i=i, gsl=gsl: h.dma_start(
                    out=a_scr[c * P:(c + 1) * P, gsl], in_=A_t[i][:, :]),
                    reads=(f"A{i}",), writes=(f"as{g}_{c}",))
                em.dma("gpsimd", lambda h, c=c, i=i, gsl=gsl: h.dma_start(
                    out=b_scr[c * P:(c + 1) * P, gsl], in_=B_t[i][:, :]),
                    reads=(f"B{i}",), writes=(f"bs{g}_{c}",))
            drain(8)

        em.fence()

        ar.reset(PRES)
        wo_sb = ar.bf16(NK * D)
        abA = [ar.bf16(T) for _ in range(2)]
        abB = [ar.bf16(T) for _ in range(2)]
        vh_bf = ar.bf16(NK * T)
        a_bf = ar.bf16(NK * T)
        sq3_t = [ar.bf16(T) for _ in range(2)]
        x_bf3 = ar.bf16(NK * T)
        bc23 = ar.bf16(2 * T)

        for kq in range(4 if PHASES >= 4 else 0):
            em.dma("gpsimd", lambda h, kq=kq: h.dma_start(
                out=wo_sb.rearrange("p (k m) -> p k m", m=D)[:, 4 * kq:4 * kq + 4, :],
                in_=woT.rearrange("(k p) m -> p k m", p=P)[:, 4 * kq:4 * kq + 4, :]),
                writes=(f"wo{kq}",))
        WOK = tuple(f"wo{i}" for i in range(4))

        VP0, VP1, PA0, PA1, SSUM, SSQ = 0, 1, 2, 3, 4, 5
        for g in range(NG if PHASES >= 4 else 0):
            gsl = slice(g * T, (g + 1) * T)
            for c in range(NK):
                i = c % 2
                em.dma("scalar", lambda h, c=c, i=i, gsl=gsl: h.dma_start(
                    out=abA[i][:, :], in_=a_scr[c * P:(c + 1) * P, gsl]),
                    reads=(f"as{g}_{c}",), writes=(f"abA{i}",))
                em.dma("scalar", lambda h, c=c, i=i, gsl=gsl: h.dma_start(
                    out=abB[i][:, :], in_=b_scr[c * P:(c + 1) * P, gsl]),
                    reads=(f"bs{g}_{c}",), writes=(f"abB{i}",))

                def invf(h, i=i):
                    h.matmul(bk(VP0 + i), tCi, abA[i][:, :], start=True, stop=False)
                    return h.matmul(bk(VP0 + i), tSi, abB[i][:, :], start=False, stop=True)
                em.op("tensor", invf,
                      reads=(f"abA{i}", f"abB{i}", "cdft"), writes=(f"vp{i}",))
                em.op("scalar", lambda h, c=c, i=i: h.copy(
                    c2(vh_bf, c), bk(VP0 + i)),
                    reads=(f"vp{i}",), writes=("vh_bf",))
            drain(10)
            w3o = wo_sb.rearrange("p (k m) -> p k m", m=D)
            for m in range(NK):
                i = m % 2

                def mmo(h, m=m, i=i):
                    last = None
                    for k in range(NK):
                        last = h.matmul(
                            bk(PA0 + i), w3o[:, k, m * P:(m + 1) * P],
                            c2(vh_bf, k), start=(k == 0), stop=(k == NK - 1))
                    return last
                em.op("tensor", mmo, reads=("vh_bf",) + WOK,
                      writes=(f"pa{i}",))
                em.op("scalar", lambda h, i=i: h.activation(
                    sq3_t[i][:, :], bk(PA0 + i), AF.Square),
                    reads=(f"pa{i}",), writes=(f"sq3{i}",))
                em.op("scalar", lambda h, m=m, i=i: h.copy(
                    c2(a_bf, m), bk(PA0 + i)),
                    reads=(f"pa{i}",), writes=("a_bf",))
                em.op("tensor", lambda h, m=m: h.matmul(
                    bkrow(SSUM), tones[:, :], c2(a_bf, m),
                    start=(m == 0), stop=(m == NK - 1)),
                    reads=("a_bf", "cones"), writes=("ssum",))
                em.op("tensor", lambda h, m=m, i=i: h.matmul(
                    bkrow(SSQ), tones[:, :], sq3_t[i][:, :],
                    start=(m == 0), stop=(m == NK - 1)),
                    reads=(f"sq3{i}", "cones"), writes=("ssq",))
            drain(10)
            ln_rows("ln2", bkrow(SSUM), bkrow(SSQ), ("ssum", "ssq"),
                    D, 2, gsl)
            bc_mm2(bc23, "bc23", bk(6), bk(7), "pbc6", "pbc7", None)
            em.dma("gpsimd", lambda h, g=g: h.dma_start(
                out=r3(x_bf3), in_=xsl(xT, g)), writes=("x_bf3",))
            em.op("vector", lambda h: h.tensor_tensor(
                r3(vh_bf), r3(a_bf),
                bc23[:, 0:T].unsqueeze(1).broadcast_to([P, NK, T]),
                op=ALU.mult),
                reads=("a_bf", "bc23"), writes=("vh_bf",))
            em.op("vector", lambda h: h.tensor_tensor(
                r3(a_bf), r3(vh_bf),
                bc23[:, T:2 * T].unsqueeze(1).broadcast_to([P, NK, T]),
                op=ALU.subtract),
                reads=("vh_bf", "bc23"), writes=("a_bf",))
            em.op("vector", lambda h: h.tensor_tensor(
                vh_bf[:, :], a_bf[:, :], x_bf3[:, :], op=ALU.add),
                reads=("a_bf", "x_bf3"), writes=("vh_bf",))
            em.dma("gpsimd", lambda h, g=g: h.dma_start(
                out=xsl(x2_scr, g), in_=r3(vh_bf)),
                reads=("vh_bf",), writes=(f"x2s{g}",))
            drain(10)

        em.fence()

        while pwork:
            drain(len(pwork))
        em.fence()
        ar.reset()
        pieces = [ar.bf16(NK * 1024) for _ in range(2)]
        h2 = ar.bf16(NMH * T)
        x3_bf = ar.bf16(NK * T)
        x2r2 = [ar.bf16(NK * T) for _ in range(2)]
        f_bf = x3_bf
        stripe = [ar.bf16(1024) for _ in range(2)]
        rt_t = [ar.bf16(T) for _ in range(2)]
        h4_t = [ar.bf16(T) for _ in range(3)]
        fx_t = [rt_t[0], rt_t[1]]
        bc24 = rows[:, 7 * T:8 * T].bitcast(BF16)
        rms_bc = rows[:, 5 * T + T // 2:6 * T].bitcast(BF16)
        out_t = [ar.f32(T) for _ in range(2)]
        r_rms = rows[0:1, 3 * T + T // 2:3 * T + T].bitcast(BF16)

        FCT = tuple(f"fct_{t}" for t in range(NMH))
        PJT = tuple(f"pjTt{m}" for m in range(NMH))

        for s in range(NG if PHASES >= 5 else 0):
            tsl = slice(s * T, (s + 1) * T)
            x2r = x2r2[s % 2]
            kx2 = f"x2r{s % 2}"
            em.dma("sync", lambda h, s=s, x2r=x2r: h.dma_start(
                out=r3(x2r), in_=xsl(x2_scr, s)),
                reads=(f"x2s{s}",), writes=(kx2,))
            ln_stats(x2r, kx2, fx_t, ("rt0", "rt1"),
                     bkrow(6), bkrow(7), "pb6", "pb7")
            ln_rows("ln3", bkrow(6), bkrow(7), ("pb6", "pb7"), D, 4, tsl)
            bc_mm2(bc24, "bc24", bk(2), bk(3), "pb2", "pb3", None)
            ln_apply(x2r, kx2, h2[:, 0:NK * T], "h2", x3_bf, "x3f", bc24, "bc24")
            pend_rms = []

            def flush_rms():
                m, ei = pend_rms.pop(0)
                em.op("tensor", lambda h, m=m, ei=ei: h.matmul(
                    bkrow(6), tones[:, :], h4_t[ei][:, :],
                    start=(m == 0), stop=(m == NMH - 1)),
                    reads=(f"h4{ei}", "cones"), writes=("pb6",))

            for oct_ in range(8):
                pi = oct_ % 2
                for k in range(NK):
                    em.dma("sync" if (k % 2 == 0 or not OPT & 4) else "scalar",
                           lambda h, pi=pi, k=k, oct_=oct_: h.dma_start(
                        out=pieces[pi][:, k * 1024:(k + 1) * 1024],
                        in_=fc_t[oct_ * 1024:(oct_ + 1) * 1024,
                                 k * P:(k + 1) * P], transpose=True),
                        reads=FCT, writes=(f"pc{pi}_{k}",))
                pks = tuple(f"pc{pi}_{k}" for k in range(NK))
                for m8 in range(8):
                    m = oct_ * 8 + m8
                    bi = m % 6
                    er = m % 2
                    ei = m % 3

                    def mm1(h, pi=pi, m8=m8, bi=bi):
                        last = None
                        for k in range(NK):
                            last = h.matmul(
                                bk(bi),
                                pieces[pi][:, k * 1024 + m8 * P:k * 1024 + (m8 + 1) * P],
                                c2(x3_bf, k), start=(k == 0), stop=(k == NK - 1))
                        return last
                    em.op("tensor", mm1, reads=("x3f",) + pks,
                          writes=(f"pb{bi}",))
                    em.op("scalar", lambda h, bi=bi, er=er: h.activation(
                        rt_t[er][:, :], bk(bi), AF.Relu),
                        reads=(f"pb{bi}",), writes=(f"rt{er}",))
                    em.op("vector", lambda h, m=m, er=er: h.tensor_tensor(
                        c2(h2, m), rt_t[er][:, :], rt_t[er][:, :],
                        op=ALU.mult),
                        reads=(f"rt{er}",), writes=("h2",))
                    em.op("vector", lambda h, m=m, ei=ei: h.tensor_tensor(
                        h4_t[ei][:, :], c2(h2, m), c2(h2, m), op=ALU.mult),
                        reads=("h2",), writes=(f"h4{ei}",))
                    pend_rms.append((m, ei))
                    if len(pend_rms) > (2 if OPT & 1 else 0):
                        flush_rms()
            while pend_rms:
                flush_rms()
            em.op("scalar", lambda h: h.activation(
                r_t1, bkrow(6), AF.Ln, scale=1.0 / HID, bias=eps_rms),
                reads=("pb6", "cmisc"), writes=("r_t1",))
            em.op("scalar", lambda h: h.activation(
                r_rms, r_t1, AF.Exp, scale=-0.5),
                reads=("r_t1",), writes=("r_rms",))
            em.op("tensor", lambda h: h.matmul(
                bk(7), onesrow, r_rms, start=True, stop=True),
                reads=("r_rms", "conesr"), writes=("pb7",))
            em.op("vector", lambda h: h.tensor_copy(
                rms_bc[:, :], bk(7)), reads=("pb7",), writes=("rms_bc",))
            for jo in range(2):
                for m in range(NMH):
                    si = m % 2
                    em.dma("sync" if (m % 2 == 0 or not OPT & 2) else "scalar",
                           lambda h, si=si, m=m, jo=jo: h.dma_start(
                        out=stripe[si][:, :],
                        in_=pjT_t[m * P:(m + 1) * P,
                                  jo * 1024:(jo + 1) * 1024]),
                        reads=PJT, writes=(f"st{si}",))

                    def mm2(h, si=si, m=m):
                        last = None
                        for j8 in range(8):
                            last = h.matmul(
                                bk(j8), stripe[si][:, j8 * P:(j8 + 1) * P],
                                c2(h2, m), start=(m == 0), stop=(m == NMH - 1))
                        return last
                    em.op("tensor", mm2, reads=(f"st{si}", "h2"),
                          writes=tuple(f"pb{j8}" for j8 in range(8)))
                for j8 in range(8):
                    j = jo * 8 + j8
                    em.op("vector", lambda h, j=j, j8=j8: h.tensor_tensor(
                        c2(f_bf, j), bk(j8), rms_bc[:, :], op=ALU.mult),
                        reads=(f"pb{j8}", "rms_bc"), writes=("x3f",))
            ln_stats(f_bf, "x3f", fx_t, ("rt0", "rt1"),
                     bkrow(6), bkrow(7), "pb6", "pb7")
            ln_rows("ln4", bkrow(6), bkrow(7), ("pb6", "pb7"), D, 6, tsl)
            bc_mm2(bc24, "bc24", bk(0), bk(1), "pb0", "pb1", None)
            for j in range(NK):
                oi = j % 2
                em.op("vector", lambda h, j=j, oi=oi: h.tensor_tensor(
                    out_t[oi][:, :], c2(f_bf, j), bc24[:, 0:T], op=ALU.mult),
                    reads=("x3f", "bc24"), writes=(f"ot{oi}",))
                em.op("vector", lambda h, oi=oi: h.tensor_tensor(
                    out_t[oi][:, :], out_t[oi][:, :], bc24[:, T:2 * T],
                    op=ALU.subtract),
                    reads=(f"ot{oi}", "bc24"), writes=(f"ot{oi}",))
                em.op("vector", lambda h, j=j, oi=oi, x2r=x2r: h.tensor_tensor(
                    out_t[oi][:, :], out_t[oi][:, :], c2(x2r, j), op=ALU.add),
                    reads=(f"ot{oi}", kx2), writes=(f"ot{oi}",))
                em.dma("gpsimd", lambda h, j=j, oi=oi, tsl=tsl: h.dma_start(
                    out=out_xT[j * P:(j + 1) * P, tsl], in_=out_t[oi][:, :]),
                    reads=(f"ot{oi}",), writes=(f"out{s}_{j}",))

        em.fence()

        block = es.enter_context(nc.Block())

        def mk(engname):
            def body(h):
                for f in em.ops[engname]:
                    f(h, sems)
            return body

        block.tensor(mk("tensor"))
        block.vector(mk("vector"))
        block.scalar(mk("scalar"))
        block.gpsimd(mk("gpsimd"))
        block.sync(mk("sync"))

    return nc


def kernel(L0_attn_x, L0_attn_W_q, L0_attn_W_kv, L0_attn_W_o,
           L0_ffn_W_fc, L0_ffn_W_proj):
    x = np.asarray(L0_attn_x, dtype=np.float32)
    B = x.shape[0]
    wq = np.asarray(L0_attn_W_q, dtype=np.float32)
    wkv = np.asarray(L0_attn_W_kv, dtype=np.float32)
    wo = np.asarray(L0_attn_W_o, dtype=np.float32)
    wfc_h = np.ascontiguousarray(np.asarray(L0_ffn_W_fc, dtype=np.float32))
    wproj_h = np.ascontiguousarray(np.asarray(L0_ffn_W_proj, dtype=np.float32))

    wqT = np.ascontiguousarray(wq.T)
    wkvT = np.ascontiguousarray(wkv.T)
    woT = np.ascontiguousarray(wo.T)

    n = np.arange(P)
    ang = 2.0 * np.pi * np.outer(n, n) / P
    bf = ml_dtypes.bfloat16
    cdft = np.concatenate([
        np.cos(ang), np.sin(ang), np.cos(ang) / P, np.sin(ang) / P,
    ], axis=1).astype(bf)
    chsel = np.zeros((P, H * H), dtype=bf)
    for c in range(H):
        chsel[:, c * H + c] = 1.0
    cones = np.ones((P, 1), dtype=bf)
    cmisc = np.zeros((P, 4), dtype=np.float32)
    cmisc[:, 0] = LN_EPS
    cmisc[:, 1] = RMS_EPS

    nc = build_nc()
    shared = {
        "wqT": wqT, "wkvT": wkvT, "woT": woT, "wfc": wfc_h, "wproj": wproj_h,
        "cdft": cdft, "chsel": chsel, "cones": cones,
        "conesr": np.ones((1, P), dtype=bf), "cmisc": cmisc,
    }
    in_maps = []
    for b in range(B):
        m = dict(shared)
        m["xT"] = np.ascontiguousarray(x[b].T)
        in_maps.append(m)
    res = run_bass_kernel_spmd(nc, in_maps, core_ids=list(range(B)))
    out = np.empty((B, SEQ, D), dtype=np.float32)
    for b in range(B):
        out[b] = res.results[b]["out_xT"].T
    return out


# version 11
# speedup vs baseline: 1.0932x; 1.0932x over previous
import numpy as np
import ml_dtypes
from contextlib import ExitStack

import concourse.bass as bass
import concourse.mybir as mybir
from concourse.bass_utils import run_bass_kernel_spmd

F32 = mybir.dt.float32
BF16 = mybir.dt.bfloat16
AF = mybir.ActivationFunctionType
ALU = mybir.AluOpType
AX = mybir.AxisListType

P = 128
D = 2048
SEQ = 2048
H = 16
NK = 16
HID = 8192
NMH = 64
T = 512
NG = SEQ // T
LN_EPS = 1e-5
RMS_EPS = float(np.finfo(np.float32).eps)
THR = 0.5 + 2.0 ** -10
ARENA = 48624
import os
PHASES = int(os.environ.get("KN_PHASES", "5"))
A1A_STEPS = int(os.environ.get("KN_A1A", "9"))
DBG = int(os.environ.get("KN_DBG", "0"))
OPT = int(os.environ.get("KN_OPT", "3"))
PROJ_STEPS = int(os.environ.get("KN_PROJ", "9"))

ENGINES = ("tensor", "vector", "scalar", "gpsimd", "sync")

PSUM_KEYS = frozenset(
    ["psum", "psq", "hq", "pq0", "pq1", "pk0", "pk1", "hkv", "qc", "qs",
     "gc", "gs", "vp0", "vp1", "pa0", "pa1", "ssum", "ssq"]
    + [f"pb{i}" for i in range(8)] + ["pbc5", "pbc6", "pbc7"])


class Em:

    NDS = 8

    def __init__(self, xkeys=PSUM_KEYS):
        self.ops = {e: [] for e in ENGINES}
        self.cnt = {e: 0 for e in ENGINES}
        self.dq = {e: [0] * self.NDS for e in ENGINES}
        self.drr = {e: 0 for e in ENGINES}
        self.last_w = {}
        self.readers = {}
        self.cur = {e: {} for e in ENGINES}
        self.xkeys = xkeys
        self.last_rd = {}

    def _need(self, eng, kind, name, count, wl):
        if count <= 0:
            return
        if kind == "e" and name == eng and eng == "tensor":
            return
        if self.cur[eng].get((kind, name), 0) >= count:
            return
        key = (kind, name)
        wl[key] = max(wl.get(key, 0), count)

    def _deps(self, eng, reads, writes):
        wl = {}
        for r in reads:
            lw = self.last_w.get(r)
            if lw:
                self._need(eng, lw[0], lw[1], lw[2], wl)
        for w in writes:
            lw = self.last_w.get(w)
            if lw:
                self._need(eng, lw[0], lw[1], lw[2], wl)
            for (k, n), c in self.readers.get(w, {}).items():
                self._need(eng, k, n, c, wl)
        for (k, n), c in wl.items():
            self.cur[eng][(k, n)] = max(self.cur[eng].get((k, n), 0), c)
        return wl

    def op(self, eng, fn, reads=(), writes=()):
        wl = self._deps(eng, reads, writes)
        for r in reads:
            if r in self.xkeys:
                lr = self.last_rd.get(r)
                if lr and lr[1] != eng:
                    self._need(eng, lr[0], lr[1], lr[2], wl)
                    self.cur[eng][(lr[0], lr[1])] = max(
                        self.cur[eng].get((lr[0], lr[1]), 0), lr[2])
        self.cnt[eng] += 1
        c = self.cnt[eng]
        for r in reads:
            if r in self.xkeys:
                self.last_rd[r] = ("e", eng, c)

        def run(h, sems, wl=wl, fn=fn, eng=eng):
            for (k, n), v in wl.items():
                h.wait_ge(sems[(k, n)], v)
            fn(h).then_inc(sems[("e", eng)], 1)

        self.ops[eng].append(run)
        for w in writes:
            self.last_w[w] = ("e", eng, c)
            self.readers[w] = {}
        for r in reads:
            self.readers.setdefault(r, {})[("e", eng)] = c

    def dma(self, q, fn, reads=(), writes=()):
        wl = self._deps(q, reads, writes)
        s = self.drr[q]
        self.drr[q] = (s + 1) % self.NDS
        self._need(q, "d", (q, s), self.dq[q][s], wl)
        self.cur[q][("d", (q, s))] = max(
            self.cur[q].get(("d", (q, s)), 0), self.dq[q][s])
        self.dq[q][s] += 16
        c = self.dq[q][s]

        def run(h, sems, wl=wl, fn=fn, q=q, s=s):
            for (k, n), v in wl.items():
                h.wait_ge(sems[(k, n)], v)
            fn(h).then_inc(sems[("d", (q, s))], 16)

        self.ops[q].append(run)
        for w in writes:
            self.last_w[w] = ("d", (q, s), c)
            self.readers[w] = {}
        for r in reads:
            self.readers.setdefault(r, {})[("d", (q, s))] = c

    def fence(self):
        snap_e = dict(self.cnt)
        snap_d = {e: list(v) for e, v in self.dq.items()}
        for e in ENGINES:
            wl = {}
            for e2 in ENGINES:
                if e2 != e:
                    self._need(e, "e", e2, snap_e[e2], wl)
                for s in range(self.NDS):
                    self._need(e, "d", (e2, s), snap_d[e2][s], wl)
            for (k, n), c in wl.items():
                self.cur[e][(k, n)] = max(self.cur[e].get((k, n), 0), c)

            def run(h, sems, wl=wl):
                for (k, n), v in wl.items():
                    h.wait_ge(sems[(k, n)], v)

            self.ops[e].append(run)


class Arena:
    def __init__(self, ap):
        self.ap = ap
        self.off = 0

    def reset(self, base=0):
        self.off = base

    def _take(self, units):
        units = (units + 15) & ~15
        o = self.off
        self.off += units
        assert self.off <= ARENA, f"arena overflow: {self.off} > {ARENA}"
        return o

    def f32(self, n):
        o = self._take(n)
        return self.ap[:, o:o + n]

    def bf16(self, n):
        o = self._take((n + 1) // 2)
        return self.ap[:, o:o + (n + 1) // 2].bitcast(BF16)[:, :n]


def build_nc():
    nc = bass.Bass("TRN2", target_bir_lowering=False)

    xT = nc.dram_tensor("xT", [D, SEQ], F32, kind="ExternalInput").ap()
    wqT = nc.dram_tensor("wqT", [D, D], F32, kind="ExternalInput").ap()
    wkvT = nc.dram_tensor("wkvT", [D, D], F32, kind="ExternalInput").ap()
    woT = nc.dram_tensor("woT", [D, D], F32, kind="ExternalInput").ap()
    wfc = nc.dram_tensor("wfc", [HID, D], F32, kind="ExternalInput").ap()
    wproj = nc.dram_tensor("wproj", [D, HID], F32, kind="ExternalInput").ap()
    cdft = nc.dram_tensor("cdft", [P, 4 * P], BF16, kind="ExternalInput").ap()
    chsel = nc.dram_tensor("chsel", [P, H * H], BF16, kind="ExternalInput").ap()
    cones = nc.dram_tensor("cones", [P, 1], BF16, kind="ExternalInput").ap()
    conesr = nc.dram_tensor("conesr", [1, P], BF16, kind="ExternalInput").ap()
    cmisc = nc.dram_tensor("cmisc", [P, 4], F32, kind="ExternalInput").ap()

    out_xT = nc.dram_tensor("out_xT", [D, SEQ], F32, kind="ExternalOutput").ap()

    _ikind = "ExternalOutput" if DBG else "Internal"
    fc_t = nc.dram_tensor("fc_t", [HID, D], BF16, kind=_ikind).ap()
    pj_t = nc.dram_tensor("pj_t", [D, HID], BF16, kind="Internal").ap()
    pjT_t = nc.dram_tensor("pjT_t", [HID, D], BF16, kind=_ikind).ap()
    x1_scr = nc.dram_tensor("x1_scr", [D, SEQ], BF16, kind=_ikind).ap()
    qn_scr = nc.dram_tensor("qn_scr", [D, SEQ], BF16, kind=_ikind).ap()
    a_scr = nc.dram_tensor("a_scr", [D, SEQ], BF16, kind=_ikind).ap()
    b_scr = nc.dram_tensor("b_scr", [D, SEQ], BF16, kind=_ikind).ap()
    x2_scr = nc.dram_tensor("x2_scr", [D, SEQ], BF16, kind=_ikind).ap()
    r_scr = nc.dram_tensor("r_scr", [10, SEQ], BF16, kind="Internal").ap()
    rq_scr = nc.dram_tensor("rq_scr", [H, SEQ], BF16, kind="Internal").ap()
    rkv_scr = nc.dram_tensor("rkv_scr", [H, SEQ], BF16, kind="Internal").ap()

    em = Em()
    es = ExitStack()
    with es:
        sems = {}
        for e in ENGINES:
            sems[("e", e)] = es.enter_context(nc.semaphore(f"se_{e}"))
            for s in range(Em.NDS):
                sems[("d", (e, s))] = es.enter_context(
                    nc.semaphore(f"sd_{e}{s}"))

        tdft = es.enter_context(nc.sbuf_tensor("tdft", [P, 4 * P], BF16))
        thsel = es.enter_context(nc.sbuf_tensor("thsel", [P, H * H], BF16))
        tones = es.enter_context(nc.sbuf_tensor("tones", [P, 1], BF16))
        tonesr = es.enter_context(nc.sbuf_tensor("tonesr", [1, P], BF16))
        tmisc = es.enter_context(nc.sbuf_tensor("tmisc", [P, 4], F32))
        carry = es.enter_context(nc.sbuf_tensor("carry", [P, H], F32))
        rows = es.enter_context(nc.sbuf_tensor("rows", [P, 8 * T], F32))
        arena_t = es.enter_context(nc.sbuf_tensor("arena", [P, ARENA], F32))
        pall = es.enter_context(nc.psum_tensor("pall", [P, 4096], F32))

        ar = Arena(arena_t[:, :])
        tCf = tdft[:, 0 * P:1 * P]
        tSf = tdft[:, 1 * P:2 * P]
        tCi = tdft[:, 2 * P:3 * P]
        tSi = tdft[:, 3 * P:4 * P]
        r_m = rows[0:1, 0 * T:1 * T]
        r_v = rows[0:1, 1 * T:2 * T]
        r_t1 = rows[0:1, 2 * T:3 * T]
        onesrow = tonesr[0:1, :]
        r_rstd = rows[0:1, 3 * T:3 * T + T // 2].bitcast(BF16)
        r_mr = rows[0:1, 4 * T:4 * T + T // 2].bitcast(BF16)
        r_h16 = rows[0:16, 6 * T:7 * T]
        r_h16b = rows[0:16, 5 * T:5 * T + T // 2].bitcast(BF16)
        eps_ln = tmisc[0:1, 0:1]
        eps_rms = tmisc[0:1, 1:2]
        zeroT = tmisc[:, 2:3].broadcast_to([P, T])

        def bk(i):
            return pall[:, i * 512:(i + 1) * 512]

        def bkrow(i):
            return pall[0:1, i * 512:(i + 1) * 512]

        def bk16(i):
            return pall[0:16, i * 512:(i + 1) * 512]

        for nm, dst, src in [("cdft", tdft, cdft), ("chsel", thsel, chsel),
                             ("cones", tones, cones), ("conesr", tonesr, conesr),
                             ("cmisc", tmisc, cmisc)]:
            em.dma("sync", lambda h, d=dst, s=src: h.dma_start(out=d[:, :], in_=s[:, :]),
                   writes=(nm,))

        def c2(tile, c):
            return tile[:, c * T:(c + 1) * T]

        def r3(tile):
            return tile.rearrange("p (c t) -> p c t", t=T)

        def xsl(src, g):
            return src.rearrange("(c p) t -> p c t", p=P)[:, :, g * T:(g + 1) * T]

        def ln_stats(src_tile, src_key, sq_bufs, sq_keys, sum_row, sq_row,
                     sum_key, sq_key):
            nb = len(sq_bufs)
            pend = []

            def flush(k, i):
                em.op("tensor", lambda h, k=k, i=i: h.matmul(
                    sq_row, tones[:, :], sq_bufs[i],
                    start=(k == 0), stop=(k == NK - 1)),
                    reads=(sq_keys[i], "cones"), writes=(sq_key,))
            for k in range(NK):
                i = k % nb
                em.op("tensor", lambda h, k=k: h.matmul(
                    sum_row, tones[:, :], c2(src_tile, k),
                    start=(k == 0), stop=(k == NK - 1)),
                    reads=(src_key, "cones"), writes=(sum_key,))
                em.op("scalar", lambda h, k=k, i=i: h.activation(
                    sq_bufs[i], c2(src_tile, k), AF.Square),
                    reads=(src_key,), writes=(sq_keys[i],))
                pend.append((k, i))
                if len(pend) > nb - 2:
                    flush(*pend.pop(0))
            while pend:
                flush(*pend.pop(0))

        def ln_rows(tag, sum_row, sq_row, st_keys, denom, scr_row, gsl):
            em.op("vector", lambda h: h.tensor_scalar(
                r_m, sum_row, 1.0 / denom, None, op0=ALU.mult),
                reads=st_keys, writes=("r_m",))
            em.op("vector", lambda h: h.tensor_tensor(
                r_t1, r_m, r_m, op=ALU.mult),
                reads=("r_m",), writes=("r_t1",))
            em.op("vector", lambda h: h.scalar_tensor_tensor(
                r_v, sq_row, 1.0 / denom, r_t1,
                op0=ALU.mult, op1=ALU.subtract),
                reads=st_keys + ("r_t1",), writes=("r_v",))
            em.op("scalar", lambda h: h.activation(
                r_t1, r_v, AF.Ln, bias=eps_ln),
                reads=("r_v", "cmisc"), writes=("r_t1",))
            em.op("scalar", lambda h: h.activation(
                r_rstd, r_t1, AF.Exp, scale=-0.5),
                reads=("r_t1",), writes=("r_rstd",))
            em.op("vector", lambda h: h.tensor_tensor(
                r_mr, r_m, r_rstd, op=ALU.mult),
                reads=("r_m", "r_rstd"), writes=("r_mr",))
            return "r"

        def bc_mm2(dst, dst_key, bank0, bank1, bk0_key, bk1_key, scr_key):
            em.op("tensor", lambda h: h.matmul(
                bank0, onesrow, r_rstd, start=True, stop=True),
                reads=("r_rstd", "conesr"), writes=(bk0_key,))
            em.op("tensor", lambda h: h.matmul(
                bank1, onesrow, r_mr, start=True, stop=True),
                reads=("r_mr", "conesr"), writes=(bk1_key,))
            em.op("vector", lambda h: h.tensor_copy(
                dst[:, 0:T], bank0), reads=(bk0_key,), writes=(dst_key,))
            em.op("vector", lambda h: h.tensor_copy(
                dst[:, T:2 * T], bank1), reads=(bk1_key,), writes=(dst_key,))

        def bc_read2(dst, dst_key, scr_row, gsl, scr_key):
            em.dma("scalar", lambda h: h.dma_start(
                out=r3(dst),
                in_=r_scr[scr_row:scr_row + 2, gsl]
                .unsqueeze(0).broadcast_to([P, 2, T])),
                reads=(scr_key,), writes=(dst_key,))

        def ln_apply(src, src_key, tmp, tmp_key, dst, dst_key, bc, bc_key, n=NK):
            em.op("vector", lambda h: h.tensor_tensor(
                r3(tmp), r3(src),
                bc[:, 0:T].unsqueeze(1).broadcast_to([P, n, T]), op=ALU.mult),
                reads=(src_key, bc_key), writes=(tmp_key,))
            em.op("vector", lambda h: h.tensor_tensor(
                r3(dst), r3(tmp),
                bc[:, T:2 * T].unsqueeze(1).broadcast_to([P, n, T]),
                op=ALU.subtract),
                reads=(tmp_key, bc_key), writes=(dst_key,))

        def proj(wsb, wkeys, src, src_key, pbks, pbkeys, dst, dst_key,
                 hsq_bank, hsq_key, sqb, sqkeys):
            w3 = wsb.rearrange("p (k m) -> p k m", m=D)
            sq_hold = [None] * NK
            pend = []

            def flush_stat():
                m, i = pend.pop(0)
                em.op("tensor", lambda h, m=m, i=i: h.matmul(
                    hsq_bank, thsel[:, m * H:(m + 1) * H], sqb[i],
                    start=(m == 0), stop=(m == NK - 1)),
                    reads=(sqkeys[i], "chsel"), writes=(hsq_key,))

            nsq = len(sqkeys)
            for m in range(NK):
                i = m % nsq

                def mmf(h, m=m, i=i):
                    last = None
                    for k in range(NK):
                        last = h.matmul(
                            pbks[i % 2], w3[:, k, m * P:(m + 1) * P],
                            c2(src, k), start=(k == 0), stop=(k == NK - 1))
                    return last
                em.op("tensor", mmf, reads=(src_key,) + wkeys,
                      writes=(pbkeys[i % 2],))
                em.op("scalar", lambda h, i=i: h.activation(
                    sqb[i], pbks[i % 2], AF.Square),
                    reads=(pbkeys[i % 2],), writes=(sqkeys[i],))
                em.op("scalar", lambda h, m=m, i=i: h.copy(
                    c2(dst, m), pbks[i % 2]),
                    reads=(pbkeys[i % 2],), writes=(dst_key,))
                pend.append((m, i))
                if len(pend) > min(2, nsq - 2) if nsq > 2 else len(pend) > 0:
                    flush_stat()
            while pend:
                flush_stat()

        def head_rsqrt(hsq_bank, hsq_key, scr, scr_key, gsl, exp_tile, exp_key):
            em.op("scalar", lambda h: h.activation(
                r_h16, hsq_bank, AF.Ln, scale=128.0),
                reads=(hsq_key,), writes=("r_h16",))
            em.op("scalar", lambda h: h.activation(
                r_h16b, r_h16, AF.Exp, scale=-0.5),
                reads=("r_h16",), writes=("r_h16b",))
            em.dma("scalar", lambda h: h.dma_start(
                out=scr[:, gsl], in_=r_h16b),
                reads=("r_h16b",), writes=(scr_key,))
            em.dma("scalar", lambda h: h.dma_start(
                out=r3(exp_tile),
                in_=scr[:, gsl].unsqueeze(0).broadcast_to([P, H, T])),
                reads=(scr_key,), writes=(exp_key,))

        ar.reset()
        p_w = [ar.f32(D) for _ in range(2)]
        p_ab = [ar.bf16(D) for _ in range(2)]
        p_sgn = [ar.bf16(D) for _ in range(2)]
        p_msk = [ar.bf16(D) for _ in range(2)]
        p_ss = [ar.bf16(D) for _ in range(2)]
        p_gs = [ar.f32(32) for _ in range(2)]
        p_s = [ar.bf16(32) for _ in range(2)]
        p_T = [ar.f32(32) for _ in range(2)]
        pt_buf = [ar.bf16(D) for _ in range(1)]
        PRES = ar.off

        ar.reset(PRES)
        wq_sb = ar.bf16(NK * D)
        x_bf1 = ar.bf16(NK * T)
        sq_t = [ar.bf16(T) for _ in range(4)]
        x1_bf = ar.bf16(NK * T)
        q_bf = ar.bf16(NK * T)
        qn_bf = x1_bf
        bc2g = [ar.bf16(2 * T) for _ in range(2)]

        for kq in range(4):
            em.dma("gpsimd", lambda h, kq=kq: h.dma_start(
                out=wq_sb.rearrange("p (k m) -> p k m", m=D)[:, 4 * kq:4 * kq + 4, :],
                in_=wqT.rearrange("(k p) m -> p k m", p=P)[:, 4 * kq:4 * kq + 4, :]),
                writes=(f"wq{kq}",))
        WQK = tuple(f"wq{i}" for i in range(4))

        def tern_tile(wap, outap, r0, c0, ldi, tag, ti):
            def thunk():
                j = ldi
                kw, kab, ksg = f"p_w{j}", f"p_ab{j}", f"p_sgn{j}"
                kms, kss = f"p_msk{j}", f"p_ss{j}"
                kgs, ks, kT = f"p_gs{j}", f"p_s{j}", f"p_T{j}"
                em.dma("sync", lambda h: h.dma_start(
                    out=p_w[j][:, :], in_=wap[r0:r0 + P, c0:c0 + D]),
                    writes=(kw,))
                em.op("scalar", lambda h: h.activation(
                    p_ab[j][:, :], p_w[j][:, :], AF.Abs),
                    reads=(kw,), writes=(kab,))
                em.op("scalar", lambda h: h.activation(
                    p_sgn[j][:, :], p_w[j][:, :], AF.Sign),
                    reads=(kw,), writes=(ksg,))
                em.op("vector", lambda h: h.tensor_reduce(
                    p_gs[j][:, :],
                    p_ab[j].rearrange("p (g e) -> p g e", e=64),
                    axis=AX.X, op=ALU.add),
                    reads=(kab,), writes=(kgs,))
                em.op("vector", lambda h: h.tensor_scalar(
                    p_s[j][:, :], p_gs[j][:, :], 1.0 / 64, 1e-8,
                    op0=ALU.mult, op1=ALU.max),
                    reads=(kgs,), writes=(ks,))
                em.op("vector", lambda h: h.tensor_scalar(
                    p_T[j][:, :], p_s[j][:, :], THR, None, op0=ALU.mult),
                    reads=(ks,), writes=(kT,))
                em.op("vector", lambda h: h.tensor_tensor(
                    p_msk[j].rearrange("p (g e) -> p g e", e=64),
                    p_ab[j].rearrange("p (g e) -> p g e", e=64),
                    p_T[j][:, :].unsqueeze(2).broadcast_to([P, 32, 64]),
                    op=ALU.is_gt),
                    reads=(kab, kT), writes=(kms,))
                em.op("vector", lambda h: h.tensor_tensor(
                    p_ss[j].rearrange("p (g e) -> p g e", e=64),
                    p_sgn[j].rearrange("p (g e) -> p g e", e=64),
                    p_s[j][:, :].unsqueeze(2).broadcast_to([P, 32, 64]),
                    op=ALU.mult),
                    reads=(ksg, ks), writes=(kss,))
                em.op("vector", lambda h: h.tensor_tensor(
                    p_ab[j][:, :], p_msk[j][:, :], p_ss[j][:, :],
                    op=ALU.mult),
                    reads=(kms, kss), writes=(kab,))
                em.dma("gpsimd", lambda h: h.dma_start(
                    out=outap[r0:r0 + P, c0:c0 + D], in_=p_ab[j][:, :]),
                    reads=(kab,), writes=(f"{tag}_{ti}",))
            return thunk

        pwork = []
        if PHASES >= 1:
            ti = 0
            for rt in range(HID // P):
                pwork.append(tern_tile(wfc, fc_t, rt * P, 0, ti % 2, "fct", rt))
                ti += 1
            tj = 0
            for rt in range(D // P):
                for ct in range(HID // D):
                    pwork.append(tern_tile(wproj, pj_t, rt * P, ct * D,
                                           ti % 2, "pjt", tj))
                    ti += 1
                    tj += 1

            def pjT_thunk(m):
                def thunk():
                    em.dma("sync", lambda h: h.dma_start(
                        out=pt_buf[0][:, :], in_=pj_t[:, m * P:(m + 1) * P],
                        transpose=True),
                        reads=tuple(f"pjt_{t}" for t in range(64)),
                        writes=("ptb",))
                    em.dma("gpsimd", lambda h: h.dma_start(
                        out=pjT_t[m * P:(m + 1) * P, :], in_=pt_buf[0][:, :]),
                        reads=("ptb",), writes=(f"pjTt{m}",))
                return thunk

            for m in range(NMH):
                pwork.append(pjT_thunk(m))

        def drain(n):
            for _ in range(min(n, len(pwork))):
                pwork.pop(0)()

        drain(10)
        ST_SUM, ST_SQ, HQ, PQ0, PQ1 = 0, 1, 2, 3, 4
        for g in range(NG if PHASES >= 2 else 0):
            gsl = slice(g * T, (g + 1) * T)
            pg = g % 2
            x_bf = x_bf1
            kx = "x_bf"
            rh_exp = x_bf
            bc2 = bc2g[pg]
            kb = f"bc2{pg}"
            em.dma("gpsimd", lambda h, g=g, x_bf=x_bf: h.dma_start(
                out=r3(x_bf), in_=xsl(xT, g)), writes=(kx,))
            if A1A_STEPS < 2:
                continue
            ln_stats(x_bf, kx, sq_t, ("sq0", "sq1", "sq2", "sq3"),
                     bkrow(ST_SUM), bkrow(ST_SQ), "psum", "psq")
            if A1A_STEPS < 3:
                continue
            ln_rows("ln1", bkrow(ST_SUM), bkrow(ST_SQ), ("psum", "psq"),
                    D, 0, gsl)
            if A1A_STEPS < 4:
                continue
            bc_mm2(bc2, kb, bk(5), bk(6), "pbc5", "pbc6", None)
            ln_apply(x_bf, kx, q_bf, "q_bf", x1_bf, "x1_bf", bc2, kb)
            drain(6)
            em.dma("gpsimd", lambda h, g=g: h.dma_start(
                out=xsl(x1_scr, g), in_=r3(x1_bf)),
                reads=("x1_bf",), writes=(f"x1s{g}",))
            if A1A_STEPS < 5:
                continue
            proj(wq_sb, WQK, x1_bf, "x1_bf", [bk(PQ0), bk(PQ1)],
                 ("pq0", "pq1"), q_bf, "q_bf", bk16(HQ), "hq", sq_t,
                 ("sq0", "sq1", "sq2", "sq3"))
            if A1A_STEPS < 6:
                continue
            drain(6)
            head_rsqrt(bk16(HQ), "hq", rq_scr, f"rq{g}", gsl, rh_exp, kx)
            em.op("vector", lambda h, rh_exp=rh_exp: h.tensor_tensor(
                qn_bf[:, :], q_bf[:, :], rh_exp[:, :], op=ALU.mult),
                reads=("q_bf", kx), writes=("x1_bf",))
            em.dma("gpsimd", lambda h, g=g: h.dma_start(
                out=xsl(qn_scr, g), in_=r3(qn_bf)),
                reads=("x1_bf",), writes=(f"qns{g}",))
            drain(6)

        em.fence()

        ar.reset(PRES)
        wkv_sb = ar.bf16(NK * D)
        x1g1 = ar.bf16(NK * T)
        kv_bf = ar.bf16(NK * T)
        kn_t = [ar.bf16(T) for _ in range(2)]
        cn_bf = ar.bf16(NK * T)
        sq2_t = [ar.bf16(T) for _ in range(2)]
        qn_t = [ar.bf16(T) for _ in range(2)]
        qc_t = ar.bf16(T)
        qs_t = ar.bf16(T)
        gc_t = ar.bf16(T)
        gs_t = ar.bf16(T)
        dt1 = ar.bf16(T)
        dt2 = ar.bf16(T)
        A_t = [ar.bf16(T) for _ in range(2)]
        B_t = [ar.bf16(T) for _ in range(2)]

        for kq in range(4 if PHASES >= 3 else 0):
            em.dma("gpsimd", lambda h, kq=kq: h.dma_start(
                out=wkv_sb.rearrange("p (k m) -> p k m", m=D)[:, 4 * kq:4 * kq + 4, :],
                in_=wkvT.rearrange("(k p) m -> p k m", p=P)[:, 4 * kq:4 * kq + 4, :]),
                writes=(f"wkv{kq}",))
        WKVK = tuple(f"wkv{i}" for i in range(4))


        em.op("vector", lambda h: h.memset(carry[:, :], 0.0), writes=("carry",))

        PK0, PK1, HKV, QC, QS, GC, GS = 0, 1, 2, 3, 4, 5, 6
        for g in range(NG if PHASES >= 3 else 0):
            gsl = slice(g * T, (g + 1) * T)
            pg = g % 2
            x1g = x1g1
            kx1 = "x1g"
            rkv_exp = x1g
            em.dma("sync", lambda h, g=g, x1g=x1g: h.dma_start(
                out=r3(x1g), in_=xsl(x1_scr, g)),
                reads=(f"x1s{g}",), writes=(kx1,))
            proj(wkv_sb, WKVK, x1g, kx1, [bk(PK0), bk(PK1)],
                 ("pk0", "pk1"), kv_bf, "kv_bf", bk16(HKV), "hkv", sq2_t,
                 ("sq20", "sq21"))
            drain(8)
            head_rsqrt(bk16(HKV), "hkv", rkv_scr, f"rkv{g}", gsl,
                       rkv_exp, kx1)
            for c in range(NK):
                ki = c % 2
                em.op("vector", lambda h, c=c, ki=ki, rkv_exp=rkv_exp:
                      h.tensor_tensor(
                    kn_t[ki][:, :], c2(kv_bf, c), c2(rkv_exp, c),
                    op=ALU.mult),
                    reads=("kv_bf", kx1), writes=(f"kn{ki}",))
                em.op("vector", lambda h, c=c, ki=ki: h.tensor_tensor_scan(
                    c2(cn_bf, c), kn_t[ki][:, :], zeroT, carry[:, c:c + 1],
                    op0=ALU.add, op1=ALU.add),
                    reads=(f"kn{ki}", "carry", "cmisc"), writes=("cn_bf",))
                em.op("vector", lambda h, c=c: h.tensor_copy(
                    carry[:, c:c + 1], c2(cn_bf, c)[:, T - 1:T]),
                    reads=("cn_bf",), writes=("carry",))
            drain(8)
            for c in range(NK):
                i = c % 2
                em.dma("scalar", lambda h, c=c, i=i, gsl=gsl: h.dma_start(
                    out=qn_t[i][:, :], in_=qn_scr[c * P:(c + 1) * P, gsl]),
                    reads=(f"qns{g}",), writes=(f"qnt{i}",))
                em.op("tensor", lambda h, i=i: h.matmul(
                    bk(QC), tCf, qn_t[i][:, :], start=True, stop=True),
                    reads=(f"qnt{i}", "cdft"), writes=("qc",))
                em.op("tensor", lambda h, i=i: h.matmul(
                    bk(QS), tSf, qn_t[i][:, :], start=True, stop=True),
                    reads=(f"qnt{i}", "cdft"), writes=("qs",))
                em.op("tensor", lambda h, c=c: h.matmul(
                    bk(GC), tCf, c2(cn_bf, c), start=True, stop=True),
                    reads=("cn_bf", "cdft"), writes=("gc",))
                em.op("tensor", lambda h, c=c: h.matmul(
                    bk(GS), tSf, c2(cn_bf, c), start=True, stop=True),
                    reads=("cn_bf", "cdft"), writes=("gs",))
                em.op("scalar", lambda h: h.copy(qc_t[:, :], bk(QC)),
                      reads=("qc",), writes=("qc_t",))
                em.op("scalar", lambda h: h.copy(qs_t[:, :], bk(QS)),
                      reads=("qs",), writes=("qs_t",))
                em.op("vector", lambda h: h.tensor_copy(gc_t[:, :], bk(GC)),
                      reads=("gc",), writes=("gc_t",))
                em.op("vector", lambda h: h.tensor_copy(gs_t[:, :], bk(GS)),
                      reads=("gs",), writes=("gs_t",))
                em.op("vector", lambda h, i=i: h.tensor_tensor(
                    A_t[i][:, :], qc_t[:, :], gc_t[:, :], op=ALU.mult),
                    reads=("qc_t", "gc_t"), writes=(f"A{i}",))
                em.op("vector", lambda h: h.tensor_tensor(
                    dt1[:, :], qs_t[:, :], gs_t[:, :], op=ALU.mult),
                    reads=("qs_t", "gs_t"), writes=("dt1",))
                em.op("vector", lambda h, i=i: h.tensor_tensor(
                    A_t[i][:, :], A_t[i][:, :], dt1[:, :], op=ALU.subtract),
                    reads=(f"A{i}", "dt1"), writes=(f"A{i}",))
                em.op("vector", lambda h, i=i: h.tensor_tensor(
                    B_t[i][:, :], qc_t[:, :], gs_t[:, :], op=ALU.mult),
                    reads=("qc_t", "gs_t"), writes=(f"B{i}",))
                em.op("vector", lambda h: h.tensor_tensor(
                    dt2[:, :], qs_t[:, :], gc_t[:, :], op=ALU.mult),
                    reads=("qs_t", "gc_t"), writes=("dt2",))
                em.op("vector", lambda h, i=i: h.tensor_tensor(
                    B_t[i][:, :], B_t[i][:, :], dt2[:, :], op=ALU.add),
                    reads=(f"B{i}", "dt2"), writes=(f"B{i}",))
                em.dma("gpsimd", lambda h, c=c, i=i, gsl=gsl: h.dma_start(
                    out=a_scr[c * P:(c + 1) * P, gsl], in_=A_t[i][:, :]),
                    reads=(f"A{i}",), writes=(f"as{g}_{c}",))
                em.dma("gpsimd", lambda h, c=c, i=i, gsl=gsl: h.dma_start(
                    out=b_scr[c * P:(c + 1) * P, gsl], in_=B_t[i][:, :]),
                    reads=(f"B{i}",), writes=(f"bs{g}_{c}",))
            drain(8)

        em.fence()

        ar.reset(PRES)
        wo_sb = ar.bf16(NK * D)
        abA = [ar.bf16(T) for _ in range(2)]
        abB = [ar.bf16(T) for _ in range(2)]
        vh_bf = ar.bf16(NK * T)
        a_bf = ar.bf16(NK * T)
        sq3_t = [ar.bf16(T) for _ in range(2)]
        x_bf3 = ar.bf16(NK * T)
        bc23 = ar.bf16(2 * T)

        for kq in range(4 if PHASES >= 4 else 0):
            em.dma("gpsimd", lambda h, kq=kq: h.dma_start(
                out=wo_sb.rearrange("p (k m) -> p k m", m=D)[:, 4 * kq:4 * kq + 4, :],
                in_=woT.rearrange("(k p) m -> p k m", p=P)[:, 4 * kq:4 * kq + 4, :]),
                writes=(f"wo{kq}",))
        WOK = tuple(f"wo{i}" for i in range(4))

        VP0, VP1, PA0, PA1, SSUM, SSQ = 0, 1, 2, 3, 4, 5
        for g in range(NG if PHASES >= 4 else 0):
            gsl = slice(g * T, (g + 1) * T)
            for c in range(NK):
                i = c % 2
                em.dma("scalar", lambda h, c=c, i=i, gsl=gsl: h.dma_start(
                    out=abA[i][:, :], in_=a_scr[c * P:(c + 1) * P, gsl]),
                    reads=(f"as{g}_{c}",), writes=(f"abA{i}",))
                em.dma("scalar", lambda h, c=c, i=i, gsl=gsl: h.dma_start(
                    out=abB[i][:, :], in_=b_scr[c * P:(c + 1) * P, gsl]),
                    reads=(f"bs{g}_{c}",), writes=(f"abB{i}",))

                def invf(h, i=i):
                    h.matmul(bk(VP0 + i), tCi, abA[i][:, :], start=True, stop=False)
                    return h.matmul(bk(VP0 + i), tSi, abB[i][:, :], start=False, stop=True)
                em.op("tensor", invf,
                      reads=(f"abA{i}", f"abB{i}", "cdft"), writes=(f"vp{i}",))
                em.op("scalar", lambda h, c=c, i=i: h.copy(
                    c2(vh_bf, c), bk(VP0 + i)),
                    reads=(f"vp{i}",), writes=("vh_bf",))
            drain(10)
            w3o = wo_sb.rearrange("p (k m) -> p k m", m=D)
            for m in range(NK):
                i = m % 2

                def mmo(h, m=m, i=i):
                    last = None
                    for k in range(NK):
                        last = h.matmul(
                            bk(PA0 + i), w3o[:, k, m * P:(m + 1) * P],
                            c2(vh_bf, k), start=(k == 0), stop=(k == NK - 1))
                    return last
                em.op("tensor", mmo, reads=("vh_bf",) + WOK,
                      writes=(f"pa{i}",))
                em.op("scalar", lambda h, i=i: h.activation(
                    sq3_t[i][:, :], bk(PA0 + i), AF.Square),
                    reads=(f"pa{i}",), writes=(f"sq3{i}",))
                em.op("scalar", lambda h, m=m, i=i: h.copy(
                    c2(a_bf, m), bk(PA0 + i)),
                    reads=(f"pa{i}",), writes=("a_bf",))
                em.op("tensor", lambda h, m=m: h.matmul(
                    bkrow(SSUM), tones[:, :], c2(a_bf, m),
                    start=(m == 0), stop=(m == NK - 1)),
                    reads=("a_bf", "cones"), writes=("ssum",))
                em.op("tensor", lambda h, m=m, i=i: h.matmul(
                    bkrow(SSQ), tones[:, :], sq3_t[i][:, :],
                    start=(m == 0), stop=(m == NK - 1)),
                    reads=(f"sq3{i}", "cones"), writes=("ssq",))
            drain(10)
            ln_rows("ln2", bkrow(SSUM), bkrow(SSQ), ("ssum", "ssq"),
                    D, 2, gsl)
            bc_mm2(bc23, "bc23", bk(6), bk(7), "pbc6", "pbc7", None)
            em.dma("gpsimd", lambda h, g=g: h.dma_start(
                out=r3(x_bf3), in_=xsl(xT, g)), writes=("x_bf3",))
            em.op("vector", lambda h: h.tensor_tensor(
                r3(vh_bf), r3(a_bf),
                bc23[:, 0:T].unsqueeze(1).broadcast_to([P, NK, T]),
                op=ALU.mult),
                reads=("a_bf", "bc23"), writes=("vh_bf",))
            em.op("vector", lambda h: h.tensor_tensor(
                r3(a_bf), r3(vh_bf),
                bc23[:, T:2 * T].unsqueeze(1).broadcast_to([P, NK, T]),
                op=ALU.subtract),
                reads=("vh_bf", "bc23"), writes=("a_bf",))
            em.op("vector", lambda h: h.tensor_tensor(
                vh_bf[:, :], a_bf[:, :], x_bf3[:, :], op=ALU.add),
                reads=("a_bf", "x_bf3"), writes=("vh_bf",))
            em.dma("gpsimd", lambda h, g=g: h.dma_start(
                out=xsl(x2_scr, g), in_=r3(vh_bf)),
                reads=("vh_bf",), writes=(f"x2s{g}",))
            drain(10)

        em.fence()

        while pwork:
            drain(len(pwork))
        em.fence()
        ar.reset()
        pieces = [ar.bf16(NK * 1024) for _ in range(2)]
        h2 = ar.bf16(NMH * T)
        x3_bf = ar.bf16(NK * T)
        x2r1 = ar.bf16(NK * T)
        f_bf = x3_bf
        stripe = [ar.bf16(1024) for _ in range(4)]
        rt_t = [ar.bf16(T) for _ in range(3)]
        h4_t = [ar.bf16(T) for _ in range(3)]
        fx_t = [rt_t[0], rt_t[1]]
        bc24 = rows[:, 7 * T:8 * T].bitcast(BF16)
        rms_bc = rows[:, 5 * T + T // 2:6 * T].bitcast(BF16)
        out_t = [ar.f32(T) for _ in range(2)]
        r_rms = rows[0:1, 3 * T + T // 2:3 * T + T].bitcast(BF16)

        FCT = tuple(f"fct_{t}" for t in range(NMH))
        PJT = tuple(f"pjTt{m}" for m in range(NMH))

        for s in range(NG if PHASES >= 5 else 0):
            tsl = slice(s * T, (s + 1) * T)
            x2r = x2r1
            kx2 = "x2r"
            em.dma("sync", lambda h, s=s, x2r=x2r: h.dma_start(
                out=r3(x2r), in_=xsl(x2_scr, s)),
                reads=(f"x2s{s}",), writes=(kx2,))
            ln_stats(x2r, kx2, fx_t, ("rt0", "rt1"),
                     bkrow(6), bkrow(7), "pb6", "pb7")
            ln_rows("ln3", bkrow(6), bkrow(7), ("pb6", "pb7"), D, 4, tsl)
            bc_mm2(bc24, "bc24", bk(2), bk(3), "pb2", "pb3", None)
            ln_apply(x2r, kx2, h2[:, 0:NK * T], "h2", x3_bf, "x3f", bc24, "bc24")
            pend_rms = []

            def flush_rms():
                m, ei = pend_rms.pop(0)
                em.op("tensor", lambda h, m=m, ei=ei: h.matmul(
                    bkrow(6), tones[:, :], h4_t[ei][:, :],
                    start=(m == 0), stop=(m == NMH - 1)),
                    reads=(f"h4{ei}", "cones"), writes=("pb6",))

            for oct_ in range(8):
                pi = oct_ % 2
                for k in range(NK):
                    em.dma("sync" if (k % 2 == 0 or not OPT & 4) else "scalar",
                           lambda h, pi=pi, k=k, oct_=oct_: h.dma_start(
                        out=pieces[pi][:, k * 1024:(k + 1) * 1024],
                        in_=fc_t[oct_ * 1024:(oct_ + 1) * 1024,
                                 k * P:(k + 1) * P], transpose=True),
                        reads=FCT, writes=(f"pc{pi}_{k}",))
                pks = tuple(f"pc{pi}_{k}" for k in range(NK))
                for m8 in range(8):
                    m = oct_ * 8 + m8
                    bi = m % 6
                    er = m % 3
                    ei = m % 3

                    def mm1(h, pi=pi, m8=m8, bi=bi):
                        last = None
                        for k in range(NK):
                            last = h.matmul(
                                bk(bi),
                                pieces[pi][:, k * 1024 + m8 * P:k * 1024 + (m8 + 1) * P],
                                c2(x3_bf, k), start=(k == 0), stop=(k == NK - 1))
                        return last
                    em.op("tensor", mm1, reads=("x3f",) + pks,
                          writes=(f"pb{bi}",))
                    em.op("scalar", lambda h, bi=bi, er=er: h.activation(
                        rt_t[er][:, :], bk(bi), AF.Relu),
                        reads=(f"pb{bi}",), writes=(f"rt{er}",))
                    em.op("vector", lambda h, m=m, er=er: h.tensor_tensor(
                        c2(h2, m), rt_t[er][:, :], rt_t[er][:, :],
                        op=ALU.mult),
                        reads=(f"rt{er}",), writes=("h2",))
                    em.op("vector", lambda h, m=m, ei=ei: h.tensor_tensor(
                        h4_t[ei][:, :], c2(h2, m), c2(h2, m), op=ALU.mult),
                        reads=("h2",), writes=(f"h4{ei}",))
                    pend_rms.append((m, ei))
                    if len(pend_rms) > (2 if OPT & 1 else 0):
                        flush_rms()
            while pend_rms:
                flush_rms()
            em.op("scalar", lambda h: h.activation(
                r_t1, bkrow(6), AF.Ln, scale=1.0 / HID, bias=eps_rms),
                reads=("pb6", "cmisc"), writes=("r_t1",))
            em.op("scalar", lambda h: h.activation(
                r_rms, r_t1, AF.Exp, scale=-0.5),
                reads=("r_t1",), writes=("r_rms",))
            em.op("tensor", lambda h: h.matmul(
                bk(7), onesrow, r_rms, start=True, stop=True),
                reads=("r_rms", "conesr"), writes=("pb7",))
            em.op("vector", lambda h: h.tensor_copy(
                rms_bc[:, :], bk(7)), reads=("pb7",), writes=("rms_bc",))
            for jo in range(2):
                for m in range(NMH):
                    si = m % 4
                    em.dma("sync" if (m % 2 == 0 or not OPT & 2) else "scalar",
                           lambda h, si=si, m=m, jo=jo: h.dma_start(
                        out=stripe[si][:, :],
                        in_=pjT_t[m * P:(m + 1) * P,
                                  jo * 1024:(jo + 1) * 1024]),
                        reads=PJT, writes=(f"st{si}",))

                    def mm2(h, si=si, m=m):
                        last = None
                        for j8 in range(8):
                            last = h.matmul(
                                bk(j8), stripe[si][:, j8 * P:(j8 + 1) * P],
                                c2(h2, m), start=(m == 0), stop=(m == NMH - 1))
                        return last
                    em.op("tensor", mm2, reads=(f"st{si}", "h2"),
                          writes=tuple(f"pb{j8}" for j8 in range(8)))
                for j8 in range(8):
                    j = jo * 8 + j8
                    em.op("vector", lambda h, j=j, j8=j8: h.tensor_tensor(
                        c2(f_bf, j), bk(j8), rms_bc[:, :], op=ALU.mult),
                        reads=(f"pb{j8}", "rms_bc"), writes=("x3f",))
            ln_stats(f_bf, "x3f", fx_t, ("rt0", "rt1"),
                     bkrow(6), bkrow(7), "pb6", "pb7")
            ln_rows("ln4", bkrow(6), bkrow(7), ("pb6", "pb7"), D, 6, tsl)
            bc_mm2(bc24, "bc24", bk(0), bk(1), "pb0", "pb1", None)
            for j in range(NK):
                oi = j % 2
                em.op("vector", lambda h, j=j, oi=oi: h.tensor_tensor(
                    out_t[oi][:, :], c2(f_bf, j), bc24[:, 0:T], op=ALU.mult),
                    reads=("x3f", "bc24"), writes=(f"ot{oi}",))
                em.op("vector", lambda h, oi=oi: h.tensor_tensor(
                    out_t[oi][:, :], out_t[oi][:, :], bc24[:, T:2 * T],
                    op=ALU.subtract),
                    reads=(f"ot{oi}", "bc24"), writes=(f"ot{oi}",))
                em.op("vector", lambda h, j=j, oi=oi, x2r=x2r: h.tensor_tensor(
                    out_t[oi][:, :], out_t[oi][:, :], c2(x2r, j), op=ALU.add),
                    reads=(f"ot{oi}", kx2), writes=(f"ot{oi}",))
                em.dma("gpsimd", lambda h, j=j, oi=oi, tsl=tsl: h.dma_start(
                    out=out_xT[j * P:(j + 1) * P, tsl], in_=out_t[oi][:, :]),
                    reads=(f"ot{oi}",), writes=(f"out{s}_{j}",))

        em.fence()

        block = es.enter_context(nc.Block())

        def mk(engname):
            def body(h):
                for f in em.ops[engname]:
                    f(h, sems)
            return body

        block.tensor(mk("tensor"))
        block.vector(mk("vector"))
        block.scalar(mk("scalar"))
        block.gpsimd(mk("gpsimd"))
        block.sync(mk("sync"))

    return nc


def kernel(L0_attn_x, L0_attn_W_q, L0_attn_W_kv, L0_attn_W_o,
           L0_ffn_W_fc, L0_ffn_W_proj):
    x = np.asarray(L0_attn_x, dtype=np.float32)
    B = x.shape[0]
    wq = np.asarray(L0_attn_W_q, dtype=np.float32)
    wkv = np.asarray(L0_attn_W_kv, dtype=np.float32)
    wo = np.asarray(L0_attn_W_o, dtype=np.float32)
    wfc_h = np.ascontiguousarray(np.asarray(L0_ffn_W_fc, dtype=np.float32))
    wproj_h = np.ascontiguousarray(np.asarray(L0_ffn_W_proj, dtype=np.float32))

    wqT = np.ascontiguousarray(wq.T)
    wkvT = np.ascontiguousarray(wkv.T)
    woT = np.ascontiguousarray(wo.T)

    n = np.arange(P)
    ang = 2.0 * np.pi * np.outer(n, n) / P
    bf = ml_dtypes.bfloat16
    cdft = np.concatenate([
        np.cos(ang), np.sin(ang), np.cos(ang) / P, np.sin(ang) / P,
    ], axis=1).astype(bf)
    chsel = np.zeros((P, H * H), dtype=bf)
    for c in range(H):
        chsel[:, c * H + c] = 1.0
    cones = np.ones((P, 1), dtype=bf)
    cmisc = np.zeros((P, 4), dtype=np.float32)
    cmisc[:, 0] = LN_EPS
    cmisc[:, 1] = RMS_EPS

    nc = build_nc()
    shared = {
        "wqT": wqT, "wkvT": wkvT, "woT": woT, "wfc": wfc_h, "wproj": wproj_h,
        "cdft": cdft, "chsel": chsel, "cones": cones,
        "conesr": np.ones((1, P), dtype=bf), "cmisc": cmisc,
    }
    in_maps = []
    for b in range(B):
        m = dict(shared)
        m["xT"] = np.ascontiguousarray(x[b].T)
        in_maps.append(m)
    res = run_bass_kernel_spmd(nc, in_maps, core_ids=list(range(B)))
    out = np.empty((B, SEQ, D), dtype=np.float32)
    for b in range(B):
        out[b] = res.results[b]["out_xT"].T
    return out


# version 12
# speedup vs baseline: 1.1144x; 1.0194x over previous
import numpy as np
import ml_dtypes
from contextlib import ExitStack

import concourse.bass as bass
import concourse.mybir as mybir
from concourse.bass_utils import run_bass_kernel_spmd

F32 = mybir.dt.float32
BF16 = mybir.dt.bfloat16
AF = mybir.ActivationFunctionType
ALU = mybir.AluOpType
AX = mybir.AxisListType

P = 128
D = 2048
SEQ = 2048
H = 16
NK = 16
HID = 8192
NMH = 64
T = 512
NG = SEQ // T
LN_EPS = 1e-5
RMS_EPS = float(np.finfo(np.float32).eps)
THR = 0.5 + 2.0 ** -10
ARENA = 48624
import os
PHASES = int(os.environ.get("KN_PHASES", "5"))
A1A_STEPS = int(os.environ.get("KN_A1A", "9"))
DBG = int(os.environ.get("KN_DBG", "0"))
OPT = int(os.environ.get("KN_OPT", "3"))
PROJ_STEPS = int(os.environ.get("KN_PROJ", "9"))

ENGINES = ("tensor", "vector", "scalar", "gpsimd", "sync")

PSUM_KEYS = frozenset(
    ["psum", "psq", "hq", "pq0", "pq1", "pk0", "pk1", "hkv", "qc", "qs",
     "gc", "gs", "vp0", "vp1", "pa0", "pa1", "ssum", "ssq"]
    + [f"pb{i}" for i in range(8)] + ["pbc5", "pbc6", "pbc7"])


class Em:

    NDS = 8

    def __init__(self, xkeys=PSUM_KEYS):
        self.ops = {e: [] for e in ENGINES}
        self.cnt = {e: 0 for e in ENGINES}
        self.dq = {e: [0] * self.NDS for e in ENGINES}
        self.drr = {e: 0 for e in ENGINES}
        self.last_w = {}
        self.readers = {}
        self.cur = {e: {} for e in ENGINES}
        self.xkeys = xkeys
        self.last_rd = {}

    def _need(self, eng, kind, name, count, wl):
        if count <= 0:
            return
        if kind == "e" and name == eng and eng == "tensor":
            return
        if self.cur[eng].get((kind, name), 0) >= count:
            return
        key = (kind, name)
        wl[key] = max(wl.get(key, 0), count)

    def _deps(self, eng, reads, writes):
        wl = {}
        for r in reads:
            lw = self.last_w.get(r)
            if lw:
                self._need(eng, lw[0], lw[1], lw[2], wl)
        for w in writes:
            lw = self.last_w.get(w)
            if lw:
                self._need(eng, lw[0], lw[1], lw[2], wl)
            for (k, n), c in self.readers.get(w, {}).items():
                self._need(eng, k, n, c, wl)
        for (k, n), c in wl.items():
            self.cur[eng][(k, n)] = max(self.cur[eng].get((k, n), 0), c)
        return wl

    def op(self, eng, fn, reads=(), writes=()):
        wl = self._deps(eng, reads, writes)
        for r in reads:
            if r in self.xkeys:
                lr = self.last_rd.get(r)
                if lr and lr[1] != eng:
                    self._need(eng, lr[0], lr[1], lr[2], wl)
                    self.cur[eng][(lr[0], lr[1])] = max(
                        self.cur[eng].get((lr[0], lr[1]), 0), lr[2])
        self.cnt[eng] += 1
        c = self.cnt[eng]
        for r in reads:
            if r in self.xkeys:
                self.last_rd[r] = ("e", eng, c)

        def run(h, sems, wl=wl, fn=fn, eng=eng):
            for (k, n), v in wl.items():
                h.wait_ge(sems[(k, n)], v)
            fn(h).then_inc(sems[("e", eng)], 1)

        self.ops[eng].append(run)
        for w in writes:
            self.last_w[w] = ("e", eng, c)
            self.readers[w] = {}
        for r in reads:
            self.readers.setdefault(r, {})[("e", eng)] = c

    def dma(self, q, fn, reads=(), writes=()):
        wl = self._deps(q, reads, writes)
        s = self.drr[q]
        self.drr[q] = (s + 1) % self.NDS
        self._need(q, "d", (q, s), self.dq[q][s], wl)
        self.cur[q][("d", (q, s))] = max(
            self.cur[q].get(("d", (q, s)), 0), self.dq[q][s])
        self.dq[q][s] += 16
        c = self.dq[q][s]

        def run(h, sems, wl=wl, fn=fn, q=q, s=s):
            for (k, n), v in wl.items():
                h.wait_ge(sems[(k, n)], v)
            fn(h).then_inc(sems[("d", (q, s))], 16)

        self.ops[q].append(run)
        for w in writes:
            self.last_w[w] = ("d", (q, s), c)
            self.readers[w] = {}
        for r in reads:
            self.readers.setdefault(r, {})[("d", (q, s))] = c

    def fence(self):
        snap_e = dict(self.cnt)
        snap_d = {e: list(v) for e, v in self.dq.items()}
        for e in ENGINES:
            wl = {}
            for e2 in ENGINES:
                if e2 != e:
                    self._need(e, "e", e2, snap_e[e2], wl)
                for s in range(self.NDS):
                    self._need(e, "d", (e2, s), snap_d[e2][s], wl)
            for (k, n), c in wl.items():
                self.cur[e][(k, n)] = max(self.cur[e].get((k, n), 0), c)

            def run(h, sems, wl=wl):
                for (k, n), v in wl.items():
                    h.wait_ge(sems[(k, n)], v)

            self.ops[e].append(run)


class Arena:
    def __init__(self, ap):
        self.ap = ap
        self.off = 0

    def reset(self, base=0):
        self.off = base

    def _take(self, units):
        units = (units + 15) & ~15
        o = self.off
        self.off += units
        assert self.off <= ARENA, f"arena overflow: {self.off} > {ARENA}"
        return o

    def f32(self, n):
        o = self._take(n)
        return self.ap[:, o:o + n]

    def bf16(self, n):
        o = self._take((n + 1) // 2)
        return self.ap[:, o:o + (n + 1) // 2].bitcast(BF16)[:, :n]


def build_nc():
    nc = bass.Bass("TRN2", target_bir_lowering=False)

    xT = nc.dram_tensor("xT", [D, SEQ], F32, kind="ExternalInput").ap()
    wqT = nc.dram_tensor("wqT", [D, D], F32, kind="ExternalInput").ap()
    wkvT = nc.dram_tensor("wkvT", [D, D], F32, kind="ExternalInput").ap()
    woT = nc.dram_tensor("woT", [D, D], F32, kind="ExternalInput").ap()
    wfc = nc.dram_tensor("wfc", [HID, D], F32, kind="ExternalInput").ap()
    wproj = nc.dram_tensor("wproj", [D, HID], F32, kind="ExternalInput").ap()
    cdft = nc.dram_tensor("cdft", [P, 4 * P], BF16, kind="ExternalInput").ap()
    chsel = nc.dram_tensor("chsel", [P, H * H], BF16, kind="ExternalInput").ap()
    cones = nc.dram_tensor("cones", [P, 1], BF16, kind="ExternalInput").ap()
    conesr = nc.dram_tensor("conesr", [1, P], BF16, kind="ExternalInput").ap()
    cmisc = nc.dram_tensor("cmisc", [P, 4], F32, kind="ExternalInput").ap()

    out_xT = nc.dram_tensor("out_xT", [D, SEQ], F32, kind="ExternalOutput").ap()

    _ikind = "ExternalOutput" if DBG else "Internal"
    fc_t = nc.dram_tensor("fc_t", [HID, D], BF16, kind=_ikind).ap()
    pj_t = nc.dram_tensor("pj_t", [D, HID], BF16, kind="Internal").ap()
    pjT_t = nc.dram_tensor("pjT_t", [HID, D], BF16, kind=_ikind).ap()
    x1_scr = nc.dram_tensor("x1_scr", [D, SEQ], BF16, kind=_ikind).ap()
    qn_scr = nc.dram_tensor("qn_scr", [D, SEQ], BF16, kind=_ikind).ap()
    a_scr = nc.dram_tensor("a_scr", [D, SEQ], BF16, kind=_ikind).ap()
    b_scr = nc.dram_tensor("b_scr", [D, SEQ], BF16, kind=_ikind).ap()
    x2_scr = nc.dram_tensor("x2_scr", [D, SEQ], BF16, kind=_ikind).ap()
    r_scr = nc.dram_tensor("r_scr", [10, SEQ], BF16, kind="Internal").ap()
    rq_scr = nc.dram_tensor("rq_scr", [H, SEQ], BF16, kind="Internal").ap()
    rkv_scr = nc.dram_tensor("rkv_scr", [H, SEQ], BF16, kind="Internal").ap()

    em = Em()
    es = ExitStack()
    with es:
        sems = {}
        for e in ENGINES:
            sems[("e", e)] = es.enter_context(nc.semaphore(f"se_{e}"))
            for s in range(Em.NDS):
                sems[("d", (e, s))] = es.enter_context(
                    nc.semaphore(f"sd_{e}{s}"))

        tdft = es.enter_context(nc.sbuf_tensor("tdft", [P, 4 * P], BF16))
        thsel = es.enter_context(nc.sbuf_tensor("thsel", [P, H * H], BF16))
        tones = es.enter_context(nc.sbuf_tensor("tones", [P, 1], BF16))
        tonesr = es.enter_context(nc.sbuf_tensor("tonesr", [1, P], BF16))
        tmisc = es.enter_context(nc.sbuf_tensor("tmisc", [P, 4], F32))
        carry = es.enter_context(nc.sbuf_tensor("carry", [P, H], F32))
        rows = es.enter_context(nc.sbuf_tensor("rows", [P, 8 * T], F32))
        arena_t = es.enter_context(nc.sbuf_tensor("arena", [P, ARENA], F32))
        pall = es.enter_context(nc.psum_tensor("pall", [P, 4096], F32))

        ar = Arena(arena_t[:, :])
        tCf = tdft[:, 0 * P:1 * P]
        tSf = tdft[:, 1 * P:2 * P]
        tCi = tdft[:, 2 * P:3 * P]
        tSi = tdft[:, 3 * P:4 * P]
        r_m = rows[0:1, 0 * T:1 * T]
        r_v = rows[0:1, 1 * T:2 * T]
        r_t1 = rows[0:1, 2 * T:3 * T]
        onesrow = tonesr[0:1, :]
        r_rstd = rows[0:1, 3 * T:3 * T + T // 2].bitcast(BF16)
        r_mr = rows[0:1, 4 * T:4 * T + T // 2].bitcast(BF16)
        r_h16 = rows[0:16, 6 * T:7 * T]
        r_h16b = rows[0:16, 5 * T:5 * T + T // 2].bitcast(BF16)
        eps_ln = tmisc[0:1, 0:1]
        eps_rms = tmisc[0:1, 1:2]
        zeroT = tmisc[:, 2:3].broadcast_to([P, T])

        def bk(i):
            return pall[:, i * 512:(i + 1) * 512]

        def bkrow(i):
            return pall[0:1, i * 512:(i + 1) * 512]

        def bk16(i):
            return pall[0:16, i * 512:(i + 1) * 512]

        for nm, dst, src in [("cdft", tdft, cdft), ("chsel", thsel, chsel),
                             ("cones", tones, cones), ("conesr", tonesr, conesr),
                             ("cmisc", tmisc, cmisc)]:
            em.dma("sync", lambda h, d=dst, s=src: h.dma_start(out=d[:, :], in_=s[:, :]),
                   writes=(nm,))

        def c2(tile, c):
            return tile[:, c * T:(c + 1) * T]

        def r3(tile):
            return tile.rearrange("p (c t) -> p c t", t=T)

        def xsl(src, g):
            return src.rearrange("(c p) t -> p c t", p=P)[:, :, g * T:(g + 1) * T]

        def ln_stats(src_tile, src_key, sq_bufs, sq_keys, sum_row, sq_row,
                     sum_key, sq_key):
            nb = len(sq_bufs)
            pend = []

            def flush(k, i):
                em.op("tensor", lambda h, k=k, i=i: h.matmul(
                    sq_row, tones[:, :], sq_bufs[i],
                    start=(k == 0), stop=(k == NK - 1)),
                    reads=(sq_keys[i], "cones"), writes=(sq_key,))
            for k in range(NK):
                i = k % nb
                em.op("tensor", lambda h, k=k: h.matmul(
                    sum_row, tones[:, :], c2(src_tile, k),
                    start=(k == 0), stop=(k == NK - 1)),
                    reads=(src_key, "cones"), writes=(sum_key,))
                em.op("scalar", lambda h, k=k, i=i: h.activation(
                    sq_bufs[i], c2(src_tile, k), AF.Square),
                    reads=(src_key,), writes=(sq_keys[i],))
                pend.append((k, i))
                if len(pend) > nb - 2:
                    flush(*pend.pop(0))
            while pend:
                flush(*pend.pop(0))

        def ln_rows(tag, sum_row, sq_row, st_keys, denom, scr_row, gsl):
            em.op("vector", lambda h: h.tensor_scalar(
                r_m, sum_row, 1.0 / denom, None, op0=ALU.mult),
                reads=st_keys, writes=("r_m",))
            em.op("vector", lambda h: h.tensor_tensor(
                r_t1, r_m, r_m, op=ALU.mult),
                reads=("r_m",), writes=("r_t1",))
            em.op("vector", lambda h: h.scalar_tensor_tensor(
                r_v, sq_row, 1.0 / denom, r_t1,
                op0=ALU.mult, op1=ALU.subtract),
                reads=st_keys + ("r_t1",), writes=("r_v",))
            em.op("scalar", lambda h: h.activation(
                r_t1, r_v, AF.Ln, bias=eps_ln),
                reads=("r_v", "cmisc"), writes=("r_t1",))
            em.op("scalar", lambda h: h.activation(
                r_rstd, r_t1, AF.Exp, scale=-0.5),
                reads=("r_t1",), writes=("r_rstd",))
            em.op("vector", lambda h: h.tensor_tensor(
                r_mr, r_m, r_rstd, op=ALU.mult),
                reads=("r_m", "r_rstd"), writes=("r_mr",))
            return "r"

        def bc_mm2(dst, dst_key, bank0, bank1, bk0_key, bk1_key, scr_key):
            em.op("tensor", lambda h: h.matmul(
                bank0, onesrow, r_rstd, start=True, stop=True),
                reads=("r_rstd", "conesr"), writes=(bk0_key,))
            em.op("tensor", lambda h: h.matmul(
                bank1, onesrow, r_mr, start=True, stop=True),
                reads=("r_mr", "conesr"), writes=(bk1_key,))
            em.op("vector", lambda h: h.tensor_copy(
                dst[:, 0:T], bank0), reads=(bk0_key,), writes=(dst_key,))
            em.op("vector", lambda h: h.tensor_copy(
                dst[:, T:2 * T], bank1), reads=(bk1_key,), writes=(dst_key,))

        def bc_read2(dst, dst_key, scr_row, gsl, scr_key):
            em.dma("scalar", lambda h: h.dma_start(
                out=r3(dst),
                in_=r_scr[scr_row:scr_row + 2, gsl]
                .unsqueeze(0).broadcast_to([P, 2, T])),
                reads=(scr_key,), writes=(dst_key,))

        def ln_apply(src, src_key, tmp, tmp_key, dst, dst_key, bc, bc_key, n=NK):
            em.op("vector", lambda h: h.tensor_tensor(
                r3(tmp), r3(src),
                bc[:, 0:T].unsqueeze(1).broadcast_to([P, n, T]), op=ALU.mult),
                reads=(src_key, bc_key), writes=(tmp_key,))
            em.op("vector", lambda h: h.tensor_tensor(
                r3(dst), r3(tmp),
                bc[:, T:2 * T].unsqueeze(1).broadcast_to([P, n, T]),
                op=ALU.subtract),
                reads=(tmp_key, bc_key), writes=(dst_key,))

        def proj(wsb, wkeys, src, src_key, pbks, pbkeys, dst, dst_key,
                 hsq_bank, hsq_key, sqb, sqkeys):
            w3 = wsb.rearrange("p (k m) -> p k m", m=D)
            sq_hold = [None] * NK
            pend = []

            def flush_stat():
                m, i = pend.pop(0)
                em.op("tensor", lambda h, m=m, i=i: h.matmul(
                    hsq_bank, thsel[:, m * H:(m + 1) * H], sqb[i],
                    start=(m == 0), stop=(m == NK - 1)),
                    reads=(sqkeys[i], "chsel"), writes=(hsq_key,))

            nsq = len(sqkeys)
            for m in range(NK):
                i = m % nsq

                def mmf(h, m=m, i=i):
                    last = None
                    for k in range(NK):
                        last = h.matmul(
                            pbks[i % 2], w3[:, k, m * P:(m + 1) * P],
                            c2(src, k), start=(k == 0), stop=(k == NK - 1))
                    return last
                em.op("tensor", mmf, reads=(src_key,) + wkeys,
                      writes=(pbkeys[i % 2],))
                em.op("scalar", lambda h, i=i: h.activation(
                    sqb[i], pbks[i % 2], AF.Square),
                    reads=(pbkeys[i % 2],), writes=(sqkeys[i],))
                em.op("vector", lambda h, m=m, i=i: h.tensor_copy(
                    c2(dst, m), pbks[i % 2]),
                    reads=(pbkeys[i % 2],), writes=(dst_key,))
                pend.append((m, i))
                if len(pend) > min(2, nsq - 2) if nsq > 2 else len(pend) > 0:
                    flush_stat()
            while pend:
                flush_stat()

        def head_rsqrt(hsq_bank, hsq_key, scr, scr_key, gsl, exp_tile, exp_key):
            em.op("scalar", lambda h: h.activation(
                r_h16, hsq_bank, AF.Ln, scale=128.0),
                reads=(hsq_key,), writes=("r_h16",))
            em.op("scalar", lambda h: h.activation(
                r_h16b, r_h16, AF.Exp, scale=-0.5),
                reads=("r_h16",), writes=("r_h16b",))
            em.dma("scalar", lambda h: h.dma_start(
                out=scr[:, gsl], in_=r_h16b),
                reads=("r_h16b",), writes=(scr_key,))
            em.dma("scalar", lambda h: h.dma_start(
                out=r3(exp_tile),
                in_=scr[:, gsl].unsqueeze(0).broadcast_to([P, H, T])),
                reads=(scr_key,), writes=(exp_key,))

        ar.reset()
        p_w = [ar.f32(D) for _ in range(2)]
        p_ab = [ar.bf16(D) for _ in range(2)]
        p_sgn = [ar.bf16(D) for _ in range(2)]
        p_msk = [ar.bf16(D) for _ in range(2)]
        p_ss = [ar.bf16(D) for _ in range(2)]
        p_gs = [ar.f32(32) for _ in range(2)]
        p_s = [ar.bf16(32) for _ in range(2)]
        p_T = [ar.f32(32) for _ in range(2)]
        pt_buf = [ar.bf16(D) for _ in range(1)]
        PRES = ar.off

        ar.reset(PRES)
        wq_sb = ar.bf16(NK * D)
        x_bf1 = ar.bf16(NK * T)
        sq_t = [ar.bf16(T) for _ in range(4)]
        x1_bf = ar.bf16(NK * T)
        q_bf = ar.bf16(NK * T)
        qn_bf = x1_bf
        bc2g = [ar.bf16(2 * T) for _ in range(2)]

        for kq in range(4):
            em.dma("gpsimd", lambda h, kq=kq: h.dma_start(
                out=wq_sb.rearrange("p (k m) -> p k m", m=D)[:, 4 * kq:4 * kq + 4, :],
                in_=wqT.rearrange("(k p) m -> p k m", p=P)[:, 4 * kq:4 * kq + 4, :]),
                writes=(f"wq{kq}",))
        WQK = tuple(f"wq{i}" for i in range(4))

        def tern_tile(wap, outap, r0, c0, ldi, tag, ti):
            def thunk():
                j = ldi
                kw, kab, ksg = f"p_w{j}", f"p_ab{j}", f"p_sgn{j}"
                kms, kss = f"p_msk{j}", f"p_ss{j}"
                kgs, ks, kT = f"p_gs{j}", f"p_s{j}", f"p_T{j}"
                em.dma("sync", lambda h: h.dma_start(
                    out=p_w[j][:, :], in_=wap[r0:r0 + P, c0:c0 + D]),
                    writes=(kw,))
                em.op("scalar", lambda h: h.activation(
                    p_ab[j][:, :], p_w[j][:, :], AF.Abs),
                    reads=(kw,), writes=(kab,))
                em.op("scalar", lambda h: h.activation(
                    p_sgn[j][:, :], p_w[j][:, :], AF.Sign),
                    reads=(kw,), writes=(ksg,))
                em.op("vector", lambda h: h.tensor_reduce(
                    p_gs[j][:, :],
                    p_ab[j].rearrange("p (g e) -> p g e", e=64),
                    axis=AX.X, op=ALU.add),
                    reads=(kab,), writes=(kgs,))
                em.op("vector", lambda h: h.tensor_scalar(
                    p_s[j][:, :], p_gs[j][:, :], 1.0 / 64, 1e-8,
                    op0=ALU.mult, op1=ALU.max),
                    reads=(kgs,), writes=(ks,))
                em.op("vector", lambda h: h.tensor_scalar(
                    p_T[j][:, :], p_s[j][:, :], THR, None, op0=ALU.mult),
                    reads=(ks,), writes=(kT,))
                em.op("vector", lambda h: h.tensor_tensor(
                    p_msk[j].rearrange("p (g e) -> p g e", e=64),
                    p_ab[j].rearrange("p (g e) -> p g e", e=64),
                    p_T[j][:, :].unsqueeze(2).broadcast_to([P, 32, 64]),
                    op=ALU.is_gt),
                    reads=(kab, kT), writes=(kms,))
                em.op("vector", lambda h: h.tensor_tensor(
                    p_ss[j].rearrange("p (g e) -> p g e", e=64),
                    p_sgn[j].rearrange("p (g e) -> p g e", e=64),
                    p_s[j][:, :].unsqueeze(2).broadcast_to([P, 32, 64]),
                    op=ALU.mult),
                    reads=(ksg, ks), writes=(kss,))
                em.op("vector", lambda h: h.tensor_tensor(
                    p_ab[j][:, :], p_msk[j][:, :], p_ss[j][:, :],
                    op=ALU.mult),
                    reads=(kms, kss), writes=(kab,))
                em.dma("gpsimd", lambda h: h.dma_start(
                    out=outap[r0:r0 + P, c0:c0 + D], in_=p_ab[j][:, :]),
                    reads=(kab,), writes=(f"{tag}_{ti}",))
            return thunk

        pwork = []
        if PHASES >= 1:
            ti = 0
            for rt in range(HID // P):
                pwork.append(tern_tile(wfc, fc_t, rt * P, 0, ti % 2, "fct", rt))
                ti += 1
            tj = 0
            for rt in range(D // P):
                for ct in range(HID // D):
                    pwork.append(tern_tile(wproj, pj_t, rt * P, ct * D,
                                           ti % 2, "pjt", tj))
                    ti += 1
                    tj += 1

            def pjT_thunk(m):
                def thunk():
                    em.dma("sync", lambda h: h.dma_start(
                        out=pt_buf[0][:, :], in_=pj_t[:, m * P:(m + 1) * P],
                        transpose=True),
                        reads=tuple(f"pjt_{t}" for t in range(64)),
                        writes=("ptb",))
                    em.dma("gpsimd", lambda h: h.dma_start(
                        out=pjT_t[m * P:(m + 1) * P, :], in_=pt_buf[0][:, :]),
                        reads=("ptb",), writes=(f"pjTt{m}",))
                return thunk

            for m in range(NMH):
                pwork.append(pjT_thunk(m))

        def drain(n):
            for _ in range(min(n, len(pwork))):
                pwork.pop(0)()

        drain(10)
        ST_SUM, ST_SQ, HQ, PQ0, PQ1 = 0, 1, 2, 3, 4
        for g in range(NG if PHASES >= 2 else 0):
            gsl = slice(g * T, (g + 1) * T)
            pg = g % 2
            x_bf = x_bf1
            kx = "x_bf"
            rh_exp = x_bf
            bc2 = bc2g[pg]
            kb = f"bc2{pg}"
            em.dma("gpsimd", lambda h, g=g, x_bf=x_bf: h.dma_start(
                out=r3(x_bf), in_=xsl(xT, g)), writes=(kx,))
            if A1A_STEPS < 2:
                continue
            ln_stats(x_bf, kx, sq_t, ("sq0", "sq1", "sq2", "sq3"),
                     bkrow(ST_SUM), bkrow(ST_SQ), "psum", "psq")
            if A1A_STEPS < 3:
                continue
            ln_rows("ln1", bkrow(ST_SUM), bkrow(ST_SQ), ("psum", "psq"),
                    D, 0, gsl)
            if A1A_STEPS < 4:
                continue
            bc_mm2(bc2, kb, bk(5), bk(6), "pbc5", "pbc6", None)
            ln_apply(x_bf, kx, q_bf, "q_bf", x1_bf, "x1_bf", bc2, kb)
            drain(6)
            em.dma("gpsimd", lambda h, g=g: h.dma_start(
                out=xsl(x1_scr, g), in_=r3(x1_bf)),
                reads=("x1_bf",), writes=(f"x1s{g}",))
            if A1A_STEPS < 5:
                continue
            proj(wq_sb, WQK, x1_bf, "x1_bf", [bk(PQ0), bk(PQ1)],
                 ("pq0", "pq1"), q_bf, "q_bf", bk16(HQ), "hq", sq_t,
                 ("sq0", "sq1", "sq2", "sq3"))
            if A1A_STEPS < 6:
                continue
            drain(6)
            head_rsqrt(bk16(HQ), "hq", rq_scr, f"rq{g}", gsl, rh_exp, kx)
            em.op("vector", lambda h, rh_exp=rh_exp: h.tensor_tensor(
                qn_bf[:, :], q_bf[:, :], rh_exp[:, :], op=ALU.mult),
                reads=("q_bf", kx), writes=("x1_bf",))
            em.dma("gpsimd", lambda h, g=g: h.dma_start(
                out=xsl(qn_scr, g), in_=r3(qn_bf)),
                reads=("x1_bf",), writes=(f"qns{g}",))
            drain(6)

        em.fence()

        ar.reset(PRES)
        wkv_sb = ar.bf16(NK * D)
        x1g1 = ar.bf16(NK * T)
        kv_bf = ar.bf16(NK * T)
        kn_t = [ar.bf16(T) for _ in range(2)]
        cn_bf = ar.bf16(NK * T)
        sq2_t = [ar.bf16(T) for _ in range(2)]
        qn_t = [ar.bf16(T) for _ in range(2)]
        qc_t = ar.bf16(T)
        qs_t = ar.bf16(T)
        gc_t = ar.bf16(T)
        gs_t = ar.bf16(T)
        dt1 = ar.bf16(T)
        dt2 = ar.bf16(T)
        A_t = [ar.bf16(T) for _ in range(2)]
        B_t = [ar.bf16(T) for _ in range(2)]

        for kq in range(4 if PHASES >= 3 else 0):
            em.dma("gpsimd", lambda h, kq=kq: h.dma_start(
                out=wkv_sb.rearrange("p (k m) -> p k m", m=D)[:, 4 * kq:4 * kq + 4, :],
                in_=wkvT.rearrange("(k p) m -> p k m", p=P)[:, 4 * kq:4 * kq + 4, :]),
                writes=(f"wkv{kq}",))
        WKVK = tuple(f"wkv{i}" for i in range(4))


        em.op("vector", lambda h: h.memset(carry[:, :], 0.0), writes=("carry",))

        PK0, PK1, HKV, QC, QS, GC, GS = 0, 1, 2, 3, 4, 5, 6
        for g in range(NG if PHASES >= 3 else 0):
            gsl = slice(g * T, (g + 1) * T)
            pg = g % 2
            x1g = x1g1
            kx1 = "x1g"
            rkv_exp = x1g
            em.dma("sync", lambda h, g=g, x1g=x1g: h.dma_start(
                out=r3(x1g), in_=xsl(x1_scr, g)),
                reads=(f"x1s{g}",), writes=(kx1,))
            proj(wkv_sb, WKVK, x1g, kx1, [bk(PK0), bk(PK1)],
                 ("pk0", "pk1"), kv_bf, "kv_bf", bk16(HKV), "hkv", sq2_t,
                 ("sq20", "sq21"))
            drain(8)
            head_rsqrt(bk16(HKV), "hkv", rkv_scr, f"rkv{g}", gsl,
                       rkv_exp, kx1)
            for c in range(NK):
                ki = c % 2
                em.op("vector", lambda h, c=c, ki=ki, rkv_exp=rkv_exp:
                      h.tensor_tensor(
                    kn_t[ki][:, :], c2(kv_bf, c), c2(rkv_exp, c),
                    op=ALU.mult),
                    reads=("kv_bf", kx1), writes=(f"kn{ki}",))
                em.op("vector", lambda h, c=c, ki=ki: h.tensor_tensor_scan(
                    c2(cn_bf, c), kn_t[ki][:, :], zeroT, carry[:, c:c + 1],
                    op0=ALU.add, op1=ALU.add),
                    reads=(f"kn{ki}", "carry", "cmisc"), writes=("cn_bf",))
                em.op("vector", lambda h, c=c: h.tensor_copy(
                    carry[:, c:c + 1], c2(cn_bf, c)[:, T - 1:T]),
                    reads=("cn_bf",), writes=("carry",))
            drain(8)
            for c in range(NK):
                i = c % 2
                em.dma("scalar", lambda h, c=c, i=i, gsl=gsl: h.dma_start(
                    out=qn_t[i][:, :], in_=qn_scr[c * P:(c + 1) * P, gsl]),
                    reads=(f"qns{g}",), writes=(f"qnt{i}",))
                em.op("tensor", lambda h, i=i: h.matmul(
                    bk(QC), tCf, qn_t[i][:, :], start=True, stop=True),
                    reads=(f"qnt{i}", "cdft"), writes=("qc",))
                em.op("tensor", lambda h, i=i: h.matmul(
                    bk(QS), tSf, qn_t[i][:, :], start=True, stop=True),
                    reads=(f"qnt{i}", "cdft"), writes=("qs",))
                em.op("tensor", lambda h, c=c: h.matmul(
                    bk(GC), tCf, c2(cn_bf, c), start=True, stop=True),
                    reads=("cn_bf", "cdft"), writes=("gc",))
                em.op("tensor", lambda h, c=c: h.matmul(
                    bk(GS), tSf, c2(cn_bf, c), start=True, stop=True),
                    reads=("cn_bf", "cdft"), writes=("gs",))
                em.op("scalar", lambda h: h.copy(qc_t[:, :], bk(QC)),
                      reads=("qc",), writes=("qc_t",))
                em.op("scalar", lambda h: h.copy(qs_t[:, :], bk(QS)),
                      reads=("qs",), writes=("qs_t",))
                em.op("vector", lambda h: h.tensor_copy(gc_t[:, :], bk(GC)),
                      reads=("gc",), writes=("gc_t",))
                em.op("vector", lambda h: h.tensor_copy(gs_t[:, :], bk(GS)),
                      reads=("gs",), writes=("gs_t",))
                em.op("vector", lambda h, i=i: h.tensor_tensor(
                    A_t[i][:, :], qc_t[:, :], gc_t[:, :], op=ALU.mult),
                    reads=("qc_t", "gc_t"), writes=(f"A{i}",))
                em.op("vector", lambda h: h.tensor_tensor(
                    dt1[:, :], qs_t[:, :], gs_t[:, :], op=ALU.mult),
                    reads=("qs_t", "gs_t"), writes=("dt1",))
                em.op("vector", lambda h, i=i: h.tensor_tensor(
                    A_t[i][:, :], A_t[i][:, :], dt1[:, :], op=ALU.subtract),
                    reads=(f"A{i}", "dt1"), writes=(f"A{i}",))
                em.op("vector", lambda h, i=i: h.tensor_tensor(
                    B_t[i][:, :], qc_t[:, :], gs_t[:, :], op=ALU.mult),
                    reads=("qc_t", "gs_t"), writes=(f"B{i}",))
                em.op("vector", lambda h: h.tensor_tensor(
                    dt2[:, :], qs_t[:, :], gc_t[:, :], op=ALU.mult),
                    reads=("qs_t", "gc_t"), writes=("dt2",))
                em.op("vector", lambda h, i=i: h.tensor_tensor(
                    B_t[i][:, :], B_t[i][:, :], dt2[:, :], op=ALU.add),
                    reads=(f"B{i}", "dt2"), writes=(f"B{i}",))
                em.dma("gpsimd", lambda h, c=c, i=i, gsl=gsl: h.dma_start(
                    out=a_scr[c * P:(c + 1) * P, gsl], in_=A_t[i][:, :]),
                    reads=(f"A{i}",), writes=(f"as{g}_{c}",))
                em.dma("gpsimd", lambda h, c=c, i=i, gsl=gsl: h.dma_start(
                    out=b_scr[c * P:(c + 1) * P, gsl], in_=B_t[i][:, :]),
                    reads=(f"B{i}",), writes=(f"bs{g}_{c}",))
            drain(8)

        em.fence()

        ar.reset(PRES)
        wo_sb = ar.bf16(NK * D)
        abA = [ar.bf16(T) for _ in range(2)]
        abB = [ar.bf16(T) for _ in range(2)]
        vh_bf = ar.bf16(NK * T)
        a_bf = ar.bf16(NK * T)
        sq3_t = [ar.bf16(T) for _ in range(2)]
        x_bf3 = ar.bf16(NK * T)
        bc23 = ar.bf16(2 * T)

        for kq in range(4 if PHASES >= 4 else 0):
            em.dma("gpsimd", lambda h, kq=kq: h.dma_start(
                out=wo_sb.rearrange("p (k m) -> p k m", m=D)[:, 4 * kq:4 * kq + 4, :],
                in_=woT.rearrange("(k p) m -> p k m", p=P)[:, 4 * kq:4 * kq + 4, :]),
                writes=(f"wo{kq}",))
        WOK = tuple(f"wo{i}" for i in range(4))

        VP0, VP1, PA0, PA1, SSUM, SSQ = 0, 1, 2, 3, 4, 5
        for g in range(NG if PHASES >= 4 else 0):
            gsl = slice(g * T, (g + 1) * T)
            for c in range(NK):
                i = c % 2
                em.dma("scalar", lambda h, c=c, i=i, gsl=gsl: h.dma_start(
                    out=abA[i][:, :], in_=a_scr[c * P:(c + 1) * P, gsl]),
                    reads=(f"as{g}_{c}",), writes=(f"abA{i}",))
                em.dma("scalar", lambda h, c=c, i=i, gsl=gsl: h.dma_start(
                    out=abB[i][:, :], in_=b_scr[c * P:(c + 1) * P, gsl]),
                    reads=(f"bs{g}_{c}",), writes=(f"abB{i}",))

                def invf(h, i=i):
                    h.matmul(bk(VP0 + i), tCi, abA[i][:, :], start=True, stop=False)
                    return h.matmul(bk(VP0 + i), tSi, abB[i][:, :], start=False, stop=True)
                em.op("tensor", invf,
                      reads=(f"abA{i}", f"abB{i}", "cdft"), writes=(f"vp{i}",))
                em.op("vector", lambda h, c=c, i=i: h.tensor_copy(
                    c2(vh_bf, c), bk(VP0 + i)),
                    reads=(f"vp{i}",), writes=("vh_bf",))
            drain(10)
            w3o = wo_sb.rearrange("p (k m) -> p k m", m=D)
            for m in range(NK):
                i = m % 2

                def mmo(h, m=m, i=i):
                    last = None
                    for k in range(NK):
                        last = h.matmul(
                            bk(PA0 + i), w3o[:, k, m * P:(m + 1) * P],
                            c2(vh_bf, k), start=(k == 0), stop=(k == NK - 1))
                    return last
                em.op("tensor", mmo, reads=("vh_bf",) + WOK,
                      writes=(f"pa{i}",))
                em.op("scalar", lambda h, i=i: h.activation(
                    sq3_t[i][:, :], bk(PA0 + i), AF.Square),
                    reads=(f"pa{i}",), writes=(f"sq3{i}",))
                em.op("vector", lambda h, m=m, i=i: h.tensor_copy(
                    c2(a_bf, m), bk(PA0 + i)),
                    reads=(f"pa{i}",), writes=("a_bf",))
                em.op("tensor", lambda h, m=m: h.matmul(
                    bkrow(SSUM), tones[:, :], c2(a_bf, m),
                    start=(m == 0), stop=(m == NK - 1)),
                    reads=("a_bf", "cones"), writes=("ssum",))
                em.op("tensor", lambda h, m=m, i=i: h.matmul(
                    bkrow(SSQ), tones[:, :], sq3_t[i][:, :],
                    start=(m == 0), stop=(m == NK - 1)),
                    reads=(f"sq3{i}", "cones"), writes=("ssq",))
            drain(10)
            ln_rows("ln2", bkrow(SSUM), bkrow(SSQ), ("ssum", "ssq"),
                    D, 2, gsl)
            bc_mm2(bc23, "bc23", bk(6), bk(7), "pbc6", "pbc7", None)
            em.dma("gpsimd", lambda h, g=g: h.dma_start(
                out=r3(x_bf3), in_=xsl(xT, g)), writes=("x_bf3",))
            em.op("vector", lambda h: h.tensor_tensor(
                r3(vh_bf), r3(a_bf),
                bc23[:, 0:T].unsqueeze(1).broadcast_to([P, NK, T]),
                op=ALU.mult),
                reads=("a_bf", "bc23"), writes=("vh_bf",))
            em.op("vector", lambda h: h.tensor_tensor(
                r3(a_bf), r3(vh_bf),
                bc23[:, T:2 * T].unsqueeze(1).broadcast_to([P, NK, T]),
                op=ALU.subtract),
                reads=("vh_bf", "bc23"), writes=("a_bf",))
            em.op("vector", lambda h: h.tensor_tensor(
                vh_bf[:, :], a_bf[:, :], x_bf3[:, :], op=ALU.add),
                reads=("a_bf", "x_bf3"), writes=("vh_bf",))
            em.dma("gpsimd", lambda h, g=g: h.dma_start(
                out=xsl(x2_scr, g), in_=r3(vh_bf)),
                reads=("vh_bf",), writes=(f"x2s{g}",))
            drain(10)

        em.fence()

        while pwork:
            drain(len(pwork))
        em.fence()
        ar.reset()
        pieces = [ar.bf16(NK * 1024) for _ in range(2)]
        h2 = ar.bf16(NMH * T)
        x3_bf = ar.bf16(NK * T)
        x2r1 = ar.bf16(NK * T)
        f_bf = x3_bf
        stripe = [ar.bf16(1024) for _ in range(4)]
        rt_t = [ar.bf16(T) for _ in range(3)]
        h4_t = [ar.bf16(T) for _ in range(3)]
        fx_t = [rt_t[0], rt_t[1]]
        bc24 = rows[:, 7 * T:8 * T].bitcast(BF16)
        rms_bc = rows[:, 5 * T + T // 2:6 * T].bitcast(BF16)
        out_t = [ar.f32(T) for _ in range(2)]
        r_rms = rows[0:1, 3 * T + T // 2:3 * T + T].bitcast(BF16)

        FCT = tuple(f"fct_{t}" for t in range(NMH))
        PJT = tuple(f"pjTt{m}" for m in range(NMH))

        for s in range(NG if PHASES >= 5 else 0):
            tsl = slice(s * T, (s + 1) * T)
            x2r = x2r1
            kx2 = "x2r"
            em.dma("sync", lambda h, s=s, x2r=x2r: h.dma_start(
                out=r3(x2r), in_=xsl(x2_scr, s)),
                reads=(f"x2s{s}",), writes=(kx2,))
            ln_stats(x2r, kx2, fx_t, ("rt0", "rt1"),
                     bkrow(6), bkrow(7), "pb6", "pb7")
            ln_rows("ln3", bkrow(6), bkrow(7), ("pb6", "pb7"), D, 4, tsl)
            bc_mm2(bc24, "bc24", bk(2), bk(3), "pb2", "pb3", None)
            ln_apply(x2r, kx2, h2[:, 0:NK * T], "h2", x3_bf, "x3f", bc24, "bc24")
            pend_rms = []

            def flush_rms():
                m, ei = pend_rms.pop(0)
                em.op("tensor", lambda h, m=m, ei=ei: h.matmul(
                    bkrow(6), tones[:, :], h4_t[ei][:, :],
                    start=(m == 0), stop=(m == NMH - 1)),
                    reads=(f"h4{ei}", "cones"), writes=("pb6",))

            for oct_ in range(8):
                pi = oct_ % 2
                for k in range(NK):
                    em.dma("sync" if (k % 2 == 0 or not OPT & 4) else "scalar",
                           lambda h, pi=pi, k=k, oct_=oct_: h.dma_start(
                        out=pieces[pi][:, k * 1024:(k + 1) * 1024],
                        in_=fc_t[oct_ * 1024:(oct_ + 1) * 1024,
                                 k * P:(k + 1) * P], transpose=True),
                        reads=FCT, writes=(f"pc{pi}_{k}",))
                pks = tuple(f"pc{pi}_{k}" for k in range(NK))
                for m8 in range(8):
                    m = oct_ * 8 + m8
                    bi = m % 6
                    er = m % 3
                    ei = m % 3

                    def mm1(h, pi=pi, m8=m8, bi=bi):
                        last = None
                        for k in range(NK):
                            last = h.matmul(
                                bk(bi),
                                pieces[pi][:, k * 1024 + m8 * P:k * 1024 + (m8 + 1) * P],
                                c2(x3_bf, k), start=(k == 0), stop=(k == NK - 1))
                        return last
                    em.op("tensor", mm1, reads=("x3f",) + pks,
                          writes=(f"pb{bi}",))
                    em.op("scalar", lambda h, bi=bi, er=er: h.activation(
                        rt_t[er][:, :], bk(bi), AF.Relu),
                        reads=(f"pb{bi}",), writes=(f"rt{er}",))
                    em.op("vector", lambda h, m=m, er=er: h.tensor_tensor(
                        c2(h2, m), rt_t[er][:, :], rt_t[er][:, :],
                        op=ALU.mult),
                        reads=(f"rt{er}",), writes=("h2",))
                    em.op("vector", lambda h, m=m, ei=ei: h.tensor_tensor(
                        h4_t[ei][:, :], c2(h2, m), c2(h2, m), op=ALU.mult),
                        reads=("h2",), writes=(f"h4{ei}",))
                    pend_rms.append((m, ei))
                    if len(pend_rms) > (2 if OPT & 1 else 0):
                        flush_rms()
            while pend_rms:
                flush_rms()
            em.op("scalar", lambda h: h.activation(
                r_t1, bkrow(6), AF.Ln, scale=1.0 / HID, bias=eps_rms),
                reads=("pb6", "cmisc"), writes=("r_t1",))
            em.op("scalar", lambda h: h.activation(
                r_rms, r_t1, AF.Exp, scale=-0.5),
                reads=("r_t1",), writes=("r_rms",))
            em.op("tensor", lambda h: h.matmul(
                bk(7), onesrow, r_rms, start=True, stop=True),
                reads=("r_rms", "conesr"), writes=("pb7",))
            em.op("vector", lambda h: h.tensor_copy(
                rms_bc[:, :], bk(7)), reads=("pb7",), writes=("rms_bc",))
            for jo in range(2):
                for m in range(NMH):
                    si = m % 4
                    em.dma("sync" if (m % 2 == 0 or not OPT & 2) else "scalar",
                           lambda h, si=si, m=m, jo=jo: h.dma_start(
                        out=stripe[si][:, :],
                        in_=pjT_t[m * P:(m + 1) * P,
                                  jo * 1024:(jo + 1) * 1024]),
                        reads=PJT, writes=(f"st{si}",))

                    def mm2(h, si=si, m=m):
                        last = None
                        for j8 in range(8):
                            last = h.matmul(
                                bk(j8), stripe[si][:, j8 * P:(j8 + 1) * P],
                                c2(h2, m), start=(m == 0), stop=(m == NMH - 1))
                        return last
                    em.op("tensor", mm2, reads=(f"st{si}", "h2"),
                          writes=tuple(f"pb{j8}" for j8 in range(8)))
                for j8 in range(8):
                    j = jo * 8 + j8
                    em.op("vector", lambda h, j=j, j8=j8: h.tensor_tensor(
                        c2(f_bf, j), bk(j8), rms_bc[:, :], op=ALU.mult),
                        reads=(f"pb{j8}", "rms_bc"), writes=("x3f",))
            ln_stats(f_bf, "x3f", fx_t, ("rt0", "rt1"),
                     bkrow(6), bkrow(7), "pb6", "pb7")
            ln_rows("ln4", bkrow(6), bkrow(7), ("pb6", "pb7"), D, 6, tsl)
            bc_mm2(bc24, "bc24", bk(0), bk(1), "pb0", "pb1", None)
            for j in range(NK):
                oi = j % 2
                em.op("vector", lambda h, j=j, oi=oi: h.tensor_tensor(
                    out_t[oi][:, :], c2(f_bf, j), bc24[:, 0:T], op=ALU.mult),
                    reads=("x3f", "bc24"), writes=(f"ot{oi}",))
                em.op("vector", lambda h, oi=oi: h.tensor_tensor(
                    out_t[oi][:, :], out_t[oi][:, :], bc24[:, T:2 * T],
                    op=ALU.subtract),
                    reads=(f"ot{oi}", "bc24"), writes=(f"ot{oi}",))
                em.op("vector", lambda h, j=j, oi=oi, x2r=x2r: h.tensor_tensor(
                    out_t[oi][:, :], out_t[oi][:, :], c2(x2r, j), op=ALU.add),
                    reads=(f"ot{oi}", kx2), writes=(f"ot{oi}",))
                em.dma("gpsimd", lambda h, j=j, oi=oi, tsl=tsl: h.dma_start(
                    out=out_xT[j * P:(j + 1) * P, tsl], in_=out_t[oi][:, :]),
                    reads=(f"ot{oi}",), writes=(f"out{s}_{j}",))

        em.fence()

        block = es.enter_context(nc.Block())

        def mk(engname):
            def body(h):
                for f in em.ops[engname]:
                    f(h, sems)
            return body

        block.tensor(mk("tensor"))
        block.vector(mk("vector"))
        block.scalar(mk("scalar"))
        block.gpsimd(mk("gpsimd"))
        block.sync(mk("sync"))

    return nc


def kernel(L0_attn_x, L0_attn_W_q, L0_attn_W_kv, L0_attn_W_o,
           L0_ffn_W_fc, L0_ffn_W_proj):
    x = np.asarray(L0_attn_x, dtype=np.float32)
    B = x.shape[0]
    wq = np.asarray(L0_attn_W_q, dtype=np.float32)
    wkv = np.asarray(L0_attn_W_kv, dtype=np.float32)
    wo = np.asarray(L0_attn_W_o, dtype=np.float32)
    wfc_h = np.ascontiguousarray(np.asarray(L0_ffn_W_fc, dtype=np.float32))
    wproj_h = np.ascontiguousarray(np.asarray(L0_ffn_W_proj, dtype=np.float32))

    wqT = np.ascontiguousarray(wq.T)
    wkvT = np.ascontiguousarray(wkv.T)
    woT = np.ascontiguousarray(wo.T)

    n = np.arange(P)
    ang = 2.0 * np.pi * np.outer(n, n) / P
    bf = ml_dtypes.bfloat16
    cdft = np.concatenate([
        np.cos(ang), np.sin(ang), np.cos(ang) / P, np.sin(ang) / P,
    ], axis=1).astype(bf)
    chsel = np.zeros((P, H * H), dtype=bf)
    for c in range(H):
        chsel[:, c * H + c] = 1.0
    cones = np.ones((P, 1), dtype=bf)
    cmisc = np.zeros((P, 4), dtype=np.float32)
    cmisc[:, 0] = LN_EPS
    cmisc[:, 1] = RMS_EPS

    nc = build_nc()
    shared = {
        "wqT": wqT, "wkvT": wkvT, "woT": woT, "wfc": wfc_h, "wproj": wproj_h,
        "cdft": cdft, "chsel": chsel, "cones": cones,
        "conesr": np.ones((1, P), dtype=bf), "cmisc": cmisc,
    }
    in_maps = []
    for b in range(B):
        m = dict(shared)
        m["xT"] = np.ascontiguousarray(x[b].T)
        in_maps.append(m)
    res = run_bass_kernel_spmd(nc, in_maps, core_ids=list(range(B)))
    out = np.empty((B, SEQ, D), dtype=np.float32)
    for b in range(B):
        out[b] = res.results[b]["out_xT"].T
    return out
